# version 10
# baseline (speedup 1.0000x reference)
import math
from contextlib import ExitStack
from functools import reduce
import operator
import numpy as np
import ml_dtypes
import concourse.bass as bass
import concourse.mybir as mybir
from concourse.bass_utils import run_bass_kernel_spmd

F32 = mybir.dt.float32
BF16 = mybir.dt.bfloat16
I32 = mybir.dt.int32
AF = mybir.ActivationFunctionType
ALU = mybir.AluOpType
AXX = mybir.AxisListType.X


class Cfg:
    def __init__(self, D=2048, S=8192, B=2, NE=7, NBLK=128, BS=64, KS=16, M=64):
        self.D, self.S, self.B, self.NE = D, S, B, NE
        self.NBLK, self.BS, self.KS, self.M = NBLK, BS, KS, M
        self.NFC = D // 128
        self.T = B * S // 8
        assert self.T == S // 4
        self.Tt = min(512, self.T)
        self.NTT = self.T // self.Tt
        self.NKC = S // 128
        self.NQB = self.T // 128
        self.KT = min(256, S)
        self.NKT = S // self.KT
        self.VSLAB = 2
        self.NSLAB = self.NKC // self.VSLAB
        self.eps = 1e-5
        self.isq = 1.0 / math.sqrt(D)


class Sem:
    __slots__ = ("h", "n", "name")
    def __init__(self, h, name):
        self.h, self.n, self.name = h, 0, name


class P:
    def __init__(self, en, eng):
        self.en, self.e = en, eng

    def op(self, en, fn, waits=(), inc=None, n=1, drain=False):
        r = None
        if en == self.en:
            for (s, v) in waits:
                self.e.wait_ge(s.h, v)
            r = fn(self.e)
            if inc is not None:
                r.then_inc(inc.h, n)
            if drain:
                self.e.drain()
        if inc is not None:
            inc.n += n
        return r

    def wait(self, en, s, v):
        if en == self.en:
            self.e.wait_ge(s.h, v)


class Arena:
    def __init__(self, nc, name):
        self.nc, self.name = nc, name
        self.tensors, self.size = [], 0

    def at(self, off, name, shape, dtype):
        per = reduce(operator.mul, list(shape[1:]), 1) * mybir.dt.size(dtype)
        assert off % 32 == 0, name
        self.size = max(self.size, off + per)
        h = self.nc.virtual_sbuf_tensor(name, list(shape), dtype)
        self.tensors.append((h, off))
        return h

    def finalize(self, es):
        handle = es.enter_context(
            self.nc.sbuf_tensor(self.name, [128, self.size], mybir.dt.uint8))
        base = self.nc.lookup_mloc(handle).addr
        for h, off in self.tensors:
            self.nc._tensor(h.name, h.shape, h.dtype, type="SB")
            mloc = self.nc.lookup_mloc(h)
            mloc.allocated = not self.nc.target_bir_lowering
            mloc.addr = base + off


def _align(x, a=32):
    return (x + a - 1) // a * a


def build_nc(cfg: Cfg):
    nc = bass.Bass("TRN2", target_bir_lowering=False)
    D, S, B, NE = cfg.D, cfg.S, cfg.B, cfg.NE
    NFC, T, Tt, NTT = cfg.NFC, cfg.T, cfg.Tt, cfg.NTT
    NKC, NQB, KT, NKT = cfg.NKC, cfg.NQB, cfg.KT, cfg.NKT
    VSLAB, NSLAB = cfg.VSLAB, cfg.NSLAB
    BS, KS, M = cfg.BS, cfg.KS, cfg.M
    NES = NE + 1
    NQBT = Tt // 128
    NKTT = Tt // KT
    W5 = min(512, D)
    NW5 = D // W5
    assert NW5 <= 4
    eps, isq = cfg.eps, cfg.isq

    x_own = nc.dram_tensor("x_own", [128, NFC, T], F32, kind="ExternalInput")
    wq_h = nc.dram_tensor("wq", [NFC, 128, D], F32, kind="ExternalInput")
    wk_h = nc.dram_tensor("wk", [NFC, 128, D], F32, kind="ExternalInput")
    wv_h = nc.dram_tensor("wv", [NFC, 128, D], F32, kind="ExternalInput")
    wo_h = nc.dram_tensor("wo", [NFC, 128, D], F32, kind="ExternalInput")
    bq_h = nc.dram_tensor("bq", [128, NFC], F32, kind="ExternalInput")
    bk_h = nc.dram_tensor("bk", [128, NFC], F32, kind="ExternalInput")
    bo_h = nc.dram_tensor("bo", [128, NFC], F32, kind="ExternalInput")
    bv_h = nc.dram_tensor("bv", [128, NFC], F32, kind="ExternalInput")
    wf_h = nc.dram_tensor("wf", [NES * 3, NFC, 128, D], F32, kind="ExternalInput")
    gw_h = nc.dram_tensor("gw", [128, NFC, NE], F32, kind="ExternalInput")
    gb_h = nc.dram_tensor("gb", [1, NE], F32, kind="ExternalInput")
    gbias_h = nc.dram_tensor("gbias", [1, NE], F32, kind="ExternalInput")
    bvals_h = nc.dram_tensor("bvals", [M, 128, (KS // 2) * BS], F32, kind="ExternalInput")
    colrow_h = nc.dram_tensor("colrow", [BS, M, 2], I32, kind="ExternalInput")
    oneh_h = nc.dram_tensor("oneh", [NE, NE * 128], F32, kind="ExternalInput")
    identb_h = nc.dram_tensor("identb", [128, 128], BF16, kind="ExternalInput")
    identf_h = nc.dram_tensor("identf", [128, 128], F32, kind="ExternalInput")

    y_h = nc.dram_tensor("y", [128, NFC, T], F32, kind="ExternalOutput")

    wqh_s = nc.dram_tensor("wqh_s", [NFC, 128, D], BF16)
    wql_s = nc.dram_tensor("wql_s", [NFC, 128, D], BF16)
    wkh_s = nc.dram_tensor("wkh_s", [NFC, 128, D], BF16)
    wkl_s = nc.dram_tensor("wkl_s", [NFC, 128, D], BF16)
    wvh_s = nc.dram_tensor("wvh_s", [NFC, 128, D], BF16)
    wvl_s = nc.dram_tensor("wvl_s", [NFC, 128, D], BF16)
    woh_s = nc.dram_tensor("woh_s", [NFC, 128, D], BF16)
    wol_s = nc.dram_tensor("wol_s", [NFC, 128, D], BF16)
    wfh_s = nc.dram_tensor("wfh_s", [NES * 3, NFC, 128, D], BF16)
    wfl_s = nc.dram_tensor("wfl_s", [NES * 3, NFC, 128, D], BF16)
    kh_loc = nc.dram_tensor("kh_loc", [NKT // 4, 128, NFC, KT], BF16)
    kl_loc = nc.dram_tensor("kl_loc", [NKT // 4, 128, NFC, KT], BF16)
    vh_loc = nc.dram_tensor("vh_loc", [NKC // 4, 128, D], BF16)
    vl_loc = nc.dram_tensor("vl_loc", [NKC // 4, 128, D], BF16)
    kh_f = nc.dram_tensor("kh_f", [NKT, 128, NFC, KT], BF16)
    kl_f = nc.dram_tensor("kl_f", [NKT, 128, NFC, KT], BF16)
    vh_f = nc.dram_tensor("vh_f", [NKC, 128, D], BF16)
    vl_f = nc.dram_tensor("vl_f", [NKC, 128, D], BF16)
    qh_s = nc.dram_tensor("qh_s", [NQB, 128, NFC, 128], BF16)
    ql_s = nc.dram_tensor("ql_s", [NQB, 128, NFC, 128], BF16)
    cxh_s = nc.dram_tensor("cxh_s", [NQB, 128, NFC, 128], BF16)
    cxl_s = nc.dram_tensor("cxl_s", [NQB, 128, NFC, 128], BF16)
    x1_s = nc.dram_tensor("x1_s", [NTT, 128, NFC, Tt], F32)
    x2_s = nc.dram_tensor("x2_s", [NTT, 128, NFC, Tt], F32)
    x3_s = nc.dram_tensor("x3_s", [NTT, 128, NFC, Tt], F32)
    z3h_loc = nc.dram_tensor("z3h_loc", [T // 128, 128, D], BF16)
    z3l_loc = nc.dram_tensor("z3l_loc", [T // 128, 128, D], BF16)
    z3h_f = nc.dram_tensor("z3h_f", [S // 128, 128, D], BF16)
    z3l_f = nc.dram_tensor("z3l_f", [S // 128, 128, D], BF16)
    s_acc = nc.dram_tensor("s_acc", [T // 128 + 1, 128, D], F32)

    es = ExitStack()
    def sb(name, shape, dtype):
        return es.enter_context(nc.sbuf_tensor(name, shape, dtype))
    def psum(name, shape, dtype=F32):
        return es.enter_context(nc.psum_tensor(name, shape, dtype))

    identb = sb("c_identb", [128, 128], BF16)
    identf = sb("c_identf", [128, 128], F32)
    ones1 = sb("c_ones1", [128, 1], F32)
    onesr = sb("c_onesr", [1, 128], F32)
    bq_sb = sb("c_bq", [128, NFC], F32)
    bk_sb = sb("c_bk", [128, NFC], F32)
    bo_sb = sb("c_bo", [128, NFC], F32)
    bv_sb = sb("c_bv", [128, NFC], F32)
    gw32 = sb("c_gw32", [128, NFC * NE], F32)
    gwh = sb("c_gwh", [128, NFC * NE], BF16)
    gwl = sb("c_gwl", [128, NFC * NE], BF16)
    gb_sb = sb("c_gb", [1, NE], F32)
    gbias_sb = sb("c_gbias", [1, NE], F32)
    gbb_sb = sb("c_gbb", [128, NE], F32)
    gbcol = sb("c_gbcol", [NE, 1], F32)
    idx_sb = sb("c_idx", [BS, 2 * M], I32)
    st128 = sb("c_st128", [128, 8], F32)
    g32 = sb("c_g32", [NE, Tt], F32)
    gt_tm = sb("c_gt_tm", [128, NQBT * NE], F32)
    grk = sb("c_grk", [128, NE], F32)
    gtmp = sb("c_gtmp", [128, NE], F32)
    gtmp2 = sb("c_gtmp2", [128, NE], F32)
    gw_fm = sb("c_gwfm", [NE, Tt], F32)
    oneh_sb = sb("c_oneh", [NE, NE * 128], F32)

    A = Arena(nc, "arena")
    o = 0
    r1 = o
    q = r1
    xa = A.at(q, "r_xa", [128, NFC * Tt], F32); q += NFC * Tt * 4
    zh = A.at(q, "r_zh", [128, NFC * Tt], BF16); q += NFC * Tt * 2
    zl = A.at(q, "r_zl", [128, NFC * Tt], BF16); q += NFC * Tt * 2
    z32 = A.at(q, "r_z32", [128, Tt], F32); q += Tt * 4
    t32a = A.at(q, "r_t32a", [128, Tt], F32); q += Tt * 4
    t32b = A.at(q, "r_t32b", [128, Tt], F32); q += Tt * 4
    stats = A.at(q, "r_stats", [1, 3 * 512], F32); q += 3 * 512 * 4
    hh = A.at(q, "r_hh", [128, NFC * Tt], BF16); q += NFC * Tt * 2
    hl = A.at(q, "r_hl", [128, NFC * Tt], BF16); q += NFC * Tt * 2
    r1_end_ln = q
    q = r1
    kth = A.at(q, "r_kth", [128, 2 * NFC * KT], BF16); q += 2 * NFC * KT * 2
    ktl = A.at(q, "r_ktl", [128, 2 * NFC * KT], BF16); q += 2 * NFC * KT * 2
    sc_sb = A.at(q, "r_sc", [128, S], F32); q += S * 4
    qh_sb = A.at(q, "r_qh", [128, NFC * 128], BF16); q += NFC * 128 * 2
    ql_sb = A.at(q, "r_ql", [128, NFC * 128], BF16); q += NFC * 128 * 2
    pbh_row = A.at(q, "r_pbh_row", [128, S], BF16); q += S * 2
    pbl_row = A.at(q, "r_pbl_row", [128, S], BF16); q += S * 2
    ptb = []
    for i in range(2):
        ptb.append(A.at(q, f"r_ptb{i}", [128, VSLAB * 256], BF16)); q += VSLAB * 512
    r1_end_att = q
    q = r1
    zpad = _align((16 + D) * 2)
    WW = (KS // 2) * BS
    zblk_h, zblk_l, wblk32, wblk_h, wblk_l = [], [], [], [], []
    for i in range(3):
        zblk_h.append(A.at(q, f"r_zblk_h{i}", [128, 16 + D], BF16)); q += zpad
    for i in range(3):
        zblk_l.append(A.at(q, f"r_zblk_l{i}", [128, 16 + D], BF16)); q += zpad
    for i in range(2):
        wblk32.append(A.at(q, f"r_wblk32{i}", [128, WW], F32)); q += WW * 4
    for i in range(2):
        wblk_h.append(A.at(q, f"r_wblk_h{i}", [128, WW], BF16)); q += WW * 2
    for i in range(2):
        wblk_l.append(A.at(q, f"r_wblk_l{i}", [128, WW], BF16)); q += WW * 2
    oblk = []
    for i in range(2):
        oblk.append(A.at(q, f"r_oblk{i}", [BS, D], F32)); q += D * 4
    r1_end_cv = q
    q = r1
    pw32, pwh, pwl = [], [], []
    for i in range(2):
        pw32.append(A.at(q, f"r_pw32_{i}", [128, 2 * D], F32)); q += 2 * D * 4
    for i in range(2):
        pwh.append(A.at(q, f"r_pwh_{i}", [128, 2 * D], BF16)); q += 2 * D * 2
    for i in range(2):
        pwl.append(A.at(q, f"r_pwl_{i}", [128, 2 * D], BF16)); q += 2 * D * 2
    r1_end_pw = q
    o = _align(max(r1_end_ln, r1_end_att, r1_end_cv, r1_end_pw))

    r2 = o
    q = r2
    fs_hi = A.at(q, "r_fs_hi", [128, 7 * D], BF16); q += 7 * D * 2
    fs_lo = A.at(q, "r_fs_lo", [128, 7 * D], BF16); q += 7 * D * 2
    r2_end_w = q
    q = r2
    vsh = A.at(q, "r_vsh", [128, 3 * VSLAB * D], BF16); q += 3 * VSLAB * D * 2
    vsl = A.at(q, "r_vsl", [128, 3 * VSLAB * D], BF16); q += 3 * VSLAB * D * 2
    r2_end_v = q
    o = _align(max(r2_end_w, r2_end_v))

    r3 = o
    acc = A.at(r3, "r_acc", [128, NFC * Tt], F32)
    ctx32 = A.at(r3, "r_ctx32", [128, D], F32)
    cth = A.at(_align(r3 + D * 4), "r_cth", [128, D], BF16)
    ctl = A.at(_align(r3 + D * 4 + D * 2), "r_ctl", [128, D], BF16)
    o = _align(max(r3 + NFC * Tt * 4, r3 + D * 4 + 2 * D * 2))

    A.finalize(es)

    PB = [psum(f"p_b{i}", [128, 512], F32) for i in range(6)]
    PT_B = psum("p_trb", [128, 1024], BF16)
    PT_F = psum("p_trf", [128, 512], F32)

    sems = []
    def mks(name):
        s = Sem(es.enter_context(nc.semaphore(name)), name)
        sems.append(s)
        return s

    sPE = mks("sPE"); sDV = mks("sDV"); sAC = mks("sAC"); sGP = mks("sGP")
    sCC = mks("sCC")
    sSt = mks("sSt"); sLd = mks("sLd")
    sXa = mks("sXa"); sXb = mks("sXb")
    sF = [mks(f"sF{i}") for i in range(4)]
    sQl = mks("sQl")
    sKa = mks("sKa"); sKb = mks("sKb")
    sVa = mks("sVa"); sVb = mks("sVb")
    sCW = [mks(f"sCW{i}") for i in range(2)]
    sPA = mks("sPA"); sPB = mks("sPB")
    sG = [mks(f"sG{i}") for i in range(3)]
    sHg = [mks(f"sHg{i}") for i in range(3)]
    sLg = [mks(f"sLg{i}") for i in range(3)]

    def program(p: P):
        for s in sems:
            s.n = 0
        tok = {}

        def use(*names):
            out = []
            for r in names:
                out.extend(tok.get(r, []))
            return out

        def setok(res, sem):
            tok[res] = [(sem, sem.n)]

        def refresh(names):
            v = [(sm, sm.n) for sm in sems if sm.n > 0]
            for r in names:
                tok[r] = list(v)

        def dma(en, out, in_, waits=(), inc=None):
            return p.op(en, lambda e: e.dma_start(out=out, in_=in_),
                        waits=waits, inc=inc, n=16)

        def mm(out, lhsT, rhs, start, stop, waits=(), inc=None):
            return p.op("tensor",
                        lambda e: e.matmul(out, lhsT, rhs, start=start, stop=stop),
                        waits=waits, inc=inc)

        def tp(out, in_, ident, waits=(), inc=None):
            return p.op("tensor", lambda e: e.transpose(out, in_, ident),
                        waits=waits, inc=inc)

        def vec(fn, waits=(), inc=None):
            return p.op("vector", fn, waits=waits, inc=inc, drain=True)

        def act(fn, waits=(), inc=None):
            return p.op("scalar", fn, waits=waits, inc=inc, drain=True)

        def gp(fn, waits=(), inc=None):
            return p.op("gpsimd", fn, waits=waits, inc=inc, drain=True)

        def mm3(out, pairs, waits_first=(), inc=None):
            n = len(pairs)
            for i, (lh, ll, rh, rl) in enumerate(pairs):
                w = waits_first if i == 0 else ()
                mm(out, lh, rh, start=(i == 0), stop=False, waits=w)
                mm(out, lh, rl, start=False, stop=False)
                mm(out, ll, rh, start=False, stop=(i == n - 1),
                   inc=(inc if i == n - 1 else None))

        def split32(dst_h, dst_l, src32, scr32=None, waits=(), inc=None):
            vec(lambda e: e.tensor_copy(out=dst_h, in_=src32), waits=waits)
            vec(lambda e: e.tensor_sub(dst_l, src32, dst_h), inc=inc)

        slot_sems = [sF[0], sF[1], sF[2], sF[3], sKa, sKb, sQl]

        def wload(hi_src, lo_src, slot, waits_extra=(), queue="sync"):
            semf = slot_sems[slot]
            dma(queue, fs_hi[:, slot*D:(slot+1)*D], hi_src,
                waits=use(f"fsb{slot}") + list(waits_extra), inc=semf)
            dma(queue, fs_lo[:, slot*D:(slot+1)*D], lo_src, inc=semf)
            return (semf, semf.n)

        dma("sync", identb[:], identb_h.ap(), inc=sQl)
        dma("sync", identf[:], identf_h.ap(), inc=sQl)
        dma("sync", bq_sb[:], bq_h.ap(), inc=sQl)
        dma("sync", bk_sb[:], bk_h.ap(), inc=sQl)
        dma("sync", bo_sb[:], bo_h.ap(), inc=sQl)
        dma("sync", bv_sb[:], bv_h.ap(), inc=sQl)
        dma("sync", gw32[:].rearrange("p (c n) -> p c n", c=NFC), gw_h.ap(), inc=sQl)
        dma("sync", gb_sb[:], gb_h.ap(), inc=sQl)
        dma("sync", gbias_sb[:], gbias_h.ap(), inc=sQl)
        dma("sync", idx_sb[:].rearrange("p (m two) -> p m two", two=2), colrow_h.ap(), inc=sQl)
        dma("sync", oneh_sb[:], oneh_h.ap(), inc=sQl)
        setup_ld = (sQl, sQl.n)
        gp(lambda e: e.memset(ones1[:], 1.0))
        gp(lambda e: e.memset(onesr[:], 1.0), inc=sGP)
        ones_tok = (sGP, sGP.n)
        split32(gwh[:], gwl[:], gw32[:], waits=[setup_ld], inc=sDV)
        mm(PT_F[:, :NE], onesr[:], gbias_sb[:, :], start=True, stop=True,
           waits=[setup_ld, ones_tok] + use("PT_F"), inc=sPE)
        vec(lambda e: e.tensor_copy(out=gbb_sb[:, :NE], in_=PT_F[:, :NE]),
            waits=[(sPE, sPE.n)], inc=sDV)
        setok("PT_F", sDV)
        tp(PT_F[:NE, :1], gb_sb[:, :NE], identf[:1, :1], waits=use("PT_F"), inc=sPE)
        vec(lambda e: e.tensor_copy(out=gbcol[:NE, :1], in_=PT_F[:NE, :1]),
            waits=[(sPE, sPE.n)], inc=sDV)
        setok("PT_F", sDV)

        pre_srcs = []
        PWB = 2 if NFC % 2 == 0 else 1
        def _pair(apw, fc):
            if PWB == 1:
                return apw[fc][:, :]
            return apw[fc:fc+PWB].rearrange("c p d -> p c d")
        def _sb2(t):
            if PWB == 1:
                return t[:, :D]
            return t[:].rearrange("p (c d) -> p c d", c=PWB)
        for (srcw, dh, dl) in ((wq_h, wqh_s, wql_s), (wk_h, wkh_s, wkl_s),
                               (wv_h, wvh_s, wvl_s), (wo_h, woh_s, wol_s)):
            for fc in range(0, NFC, PWB):
                pre_srcs.append((_pair(srcw.ap(), fc), _pair(dh.ap(), fc),
                                 _pair(dl.ap(), fc)))
        for mi in range(NES * 3):
            for fc in range(0, NFC, PWB):
                pre_srcs.append((_pair(wf_h.ap()[mi], fc), _pair(wfh_s.ap()[mi], fc),
                                 _pair(wfl_s.ap()[mi], fc)))
        for i, (src, dh, dl) in enumerate(pre_srcs):
            slot = i % 2
            qd = "sync" if i % 2 == 0 else "scalar"
            lsem = sXa if i % 2 == 0 else sXb
            stsem = sPA if i % 2 == 0 else sPB
            st_gate = [(stsem, stsem.n)] if stsem.n > 0 else []
            dma(qd, _sb2(pw32[slot]), src, waits=use(f"pw32_{slot}"), inc=lsem)
            ld_tok = (lsem, lsem.n)
            act(lambda e, slot=slot: e.activation(out=pwh[slot][:, :],
                                                  in_=pw32[slot][:, :], func=AF.Copy),
                waits=[ld_tok] + st_gate, inc=sAC)
            cast_tok = (sAC, sAC.n)
            if i % 2 == 0:
                vec(lambda e, slot=slot: e.tensor_sub(pwl[slot][:, :], pw32[slot][:, :],
                                                      pwh[slot][:, :]),
                    waits=[cast_tok], inc=sDV)
                sp_tok = (sDV, sDV.n)
            else:
                gp(lambda e, slot=slot: e.tensor_sub(pwl[slot][:, :], pw32[slot][:, :],
                                                     pwh[slot][:, :]),
                   waits=[cast_tok], inc=sGP)
                sp_tok = (sGP, sGP.n)
            tok[f"pw32_{slot}"] = [sp_tok]
            dma(qd, dh, _sb2(pwh[slot]), waits=[sp_tok], inc=stsem)
            dma(qd, dl, _sb2(pwl[slot]), inc=stsem)
        pre_done = (sPA, sPA.n)
        pre_done_b = (sPB, sPB.n)
        refresh(["xa"])

        def layernorm(xs, W, xs_tok):
            sq_prev = None
            for c in range(NFC):
                act(lambda e, c=c: e.activation(out=z32[:, :W], in_=xs[:, c*W:(c+1)*W],
                                                func=AF.Square),
                    waits=(([xs_tok] + use("z32sq")) if c == 0 else [(sPE, sq_prev)]),
                    inc=sAC)
                mm(PB[4][:1, :W], ones1[:], xs[:, c*W:(c+1)*W],
                   start=(c == 0), stop=(c == NFC - 1),
                   waits=(([xs_tok, ones_tok] + use("PB4", "PB5")) if c == 0 else []))
                mm(PB[5][:1, :W], ones1[:], z32[:, :W],
                   start=(c == 0), stop=(c == NFC - 1),
                   waits=[(sAC, sAC.n)], inc=sPE)
                sq_prev = sPE.n
            setok("z32sq", sPE)
            pe_stats = (sPE, sPE.n)
            vec(lambda e: e.tensor_scalar_mul(stats[:1, 0:W], PB[4][:1, :W], 1.0 / D),
                waits=[pe_stats])
            vec(lambda e: e.tensor_scalar_mul(stats[:1, 512:512+W], PB[5][:1, :W], 1.0 / D))
            vec(lambda e: e.tensor_tensor(stats[:1, 1024:1024+W], stats[:1, 0:W],
                                          stats[:1, 0:W], ALU.mult))
            vec(lambda e: e.tensor_sub(stats[:1, 512:512+W], stats[:1, 512:512+W],
                                       stats[:1, 1024:1024+W]))
            vec(lambda e: e.tensor_scalar_add(stats[:1, 512:512+W], stats[:1, 512:512+W], eps),
                inc=sDV)
            setok("PB4", sDV); setok("PB5", sDV)
            act(lambda e: e.activation(out=stats[:1, 1024:1024+W], in_=stats[:1, 512:512+W],
                                       func=AF.Sqrt),
                waits=[(sDV, sDV.n)], inc=sAC)
            vec(lambda e: e.reciprocal(stats[:1, 512:512+W], stats[:1, 1024:1024+W]),
                waits=[(sAC, sAC.n)])
            vec(lambda e: e.tensor_tensor(stats[:1, 1024:1024+W], stats[:1, 0:W],
                                          stats[:1, 512:512+W], ALU.mult))
            vec(lambda e: e.tensor_scalar_mul(stats[:1, 1024:1024+W], stats[:1, 1024:1024+W],
                                              -1.0), inc=sDV)
            assert W <= 512
            mm(PB[2][:, :W], onesr[:], stats[:1, 512:512+W], start=True, stop=True,
               waits=[(sDV, sDV.n)] + use("PB2", "PB3"))
            mm(PB[3][:, :W], onesr[:], stats[:1, 1024:1024+W], start=True, stop=True, inc=sPE)
            pe_bc = (sPE, sPE.n)
            for c in range(NFC):
                vec(lambda e, c=c: e.tensor_tensor(z32[:, :W], xs[:, c*W:(c+1)*W],
                                                   PB[2][:, :W], ALU.mult),
                    waits=(([pe_bc] + use("zh")) if c == 0 else []))
                vec(lambda e, c=c: e.tensor_add(z32[:, :W], z32[:, :W], PB[3][:, :W]))
                vec(lambda e, c=c: e.tensor_copy(out=zh[:, c*W:(c+1)*W], in_=z32[:, :W]))
                vec(lambda e, c=c: e.tensor_sub(zl[:, c*W:(c+1)*W], z32[:, :W],
                                                zh[:, c*W:(c+1)*W]),
                    inc=(sDV if c == NFC - 1 else None))
            setok("PB2", sDV); setok("PB3", sDV); setok("z32sq", sDV)
            return (sDV, sDV.n)

        def p0_all():
            for tt in range(NTT):
                par = tt % 2
                xsem = sXa if par == 0 else sXb
                dma("sync", xa[:].rearrange("p (c t) -> p c t", c=NFC),
                    x_own.ap()[:, :, tt*Tt:(tt+1)*Tt], waits=use("xa"), inc=xsem)
                xtok = (xsem, xsem.n)
                ztok = layernorm(xa, Tt, xtok)
                setok("xa", sDV)
                for kind in ("q", "k", "v"):
                    w_hi, w_lo = {"q": (wqh_s, wql_s), "k": (wkh_s, wkl_s),
                                  "v": (wvh_s, wvl_s)}[kind]
                    b_sb = {"q": bq_sb, "k": bk_sb, "v": bv_sb}[kind]
                    ld_toks = {}
                    for pf in range(min(3, NFC)):
                        extra = ([pre_done, pre_done_b]
                                 if (tt == 0 and kind == "q" and pf == 0) else [])
                        ld_toks[pf] = wload(w_hi.ap()[pf][:, :], w_lo.ap()[pf][:, :],
                                            pf, waits_extra=extra)
                    for fc in range(NFC):
                        slot = fc % 4
                        if fc + 3 < NFC:
                            ld_toks[fc + 3] = wload(w_hi.ap()[fc+3][:, :],
                                                    w_lo.ap()[fc+3][:, :], (fc + 3) % 4)
                        pbk = PB[fc % 2]
                        wts = use(f"PB{fc % 2}") + [ld_toks[fc]] + ([ztok] if fc == 0 else [])
                        pairs = [(fs_hi[:, slot*D + c*128: slot*D + c*128 + 128],
                                  fs_lo[:, slot*D + c*128: slot*D + c*128 + 128],
                                  zh[:, c*Tt:(c+1)*Tt], zl[:, c*Tt:(c+1)*Tt])
                                 for c in range(NFC)]
                        mm3(pbk[:, :Tt], pairs, waits_first=wts, inc=sPE)
                        setok(f"fsb{slot}", sPE)
                        vec(lambda e, pbk=pbk, fc=fc, b_sb=b_sb:
                            e.tensor_scalar_add(t32a[:, :Tt], pbk[:, :Tt], b_sb[:, fc:fc+1]),
                            waits=[(sPE, sPE.n)] + use("t32a", "hhl"))
                        split32(hh[:, fc*Tt:(fc+1)*Tt], hl[:, fc*Tt:(fc+1)*Tt],
                                t32a[:, :Tt], inc=sDV)
                        setok(f"PB{fc % 2}", sDV)
                        setok("t32a", sDV)
                    done = (sDV, sDV.n)
                    if kind == "q":
                        for qs in range(NQBT):
                            qb = tt * NQBT + qs
                            dma("gpsimd", qh_s.ap()[qb][:, :, :],
                                hh[:].rearrange("p (c t) -> p c t", c=NFC)[:, :, qs*128:(qs+1)*128],
                                waits=([done] if qs == 0 else []), inc=sSt)
                            dma("gpsimd", ql_s.ap()[qb][:, :, :],
                                hl[:].rearrange("p (c t) -> p c t", c=NFC)[:, :, qs*128:(qs+1)*128],
                                inc=sSt)
                        tok["hhl"] = [(sSt, sSt.n)]
                    elif kind == "k":
                        for ks_ in range(NKTT):
                            kt = tt * NKTT + ks_
                            dma("gpsimd", kh_loc.ap()[kt][:, :, :],
                                hh[:].rearrange("p (c t) -> p c t", c=NFC)[:, :, ks_*KT:(ks_+1)*KT],
                                waits=([done] if ks_ == 0 else []), inc=sSt)
                            dma("gpsimd", kl_loc.ap()[kt][:, :, :],
                                hl[:].rearrange("p (c t) -> p c t", c=NFC)[:, :, ks_*KT:(ks_+1)*KT],
                                inc=sSt)
                        tok["hhl"] = [(sSt, sSt.n)]
                        if tt == NTT - 1:
                            p.wait("gpsimd", sSt, sSt.n)
                            for (csrc, cdst) in ((kh_loc, kh_f), (kl_loc, kl_f)):
                                p.op("gpsimd",
                                     lambda e, csrc=csrc, cdst=cdst: e.collective_compute(
                                         "AllGather", ALU.bypass,
                                         replica_groups=[[0, 1, 2, 3], [4, 5, 6, 7]],
                                         ins=[csrc.ap().opt()], outs=[cdst.ap().opt()]),
                                     inc=sCC)
                    else:
                        for ts in range(NQBT):
                            kc = tt * NQBT + ts
                            for fc in range(NFC):
                                tp(PT_B[:, :128],
                                   hh[:, fc*Tt + ts*128: fc*Tt + ts*128 + 128], identb[:],
                                   waits=(([done] if (ts == 0 and fc == 0) else []) +
                                          use("PT_B")))
                                tp(PT_B[:, 128:256],
                                   hl[:, fc*Tt + ts*128: fc*Tt + ts*128 + 128], identb[:],
                                   inc=sPE)
                                vec(lambda e, fc=fc: e.tensor_copy(
                                        out=cth[:, fc*128:(fc+1)*128], in_=PT_B[:, :128]),
                                    waits=[(sPE, sPE.n)] + (use("cthl") if fc == 0 else []))
                                vec(lambda e, fc=fc: e.tensor_copy(
                                        out=ctl[:, fc*128:(fc+1)*128], in_=PT_B[:, 128:256]),
                                    inc=sDV)
                                setok("PT_B", sDV)
                            vtok = (sDV, sDV.n)
                            dma("gpsimd", vh_loc.ap()[kc][:, :], cth[:, :],
                                waits=[vtok], inc=sSt)
                            dma("gpsimd", vl_loc.ap()[kc][:, :], ctl[:, :], inc=sSt)
                            tok["cthl"] = [(sSt, sSt.n)]
                        tok["hhl"] = [(sPE, sPE.n)]

        p0_all()
        p0_store_n = sSt.n

        k_cc = (sCC, sCC.n)
        p.wait("gpsimd", sSt, p0_store_n)
        for (src, dst) in ((vh_loc, vh_f), (vl_loc, vl_f)):
            p.op("gpsimd",
                 lambda e, src=src, dst=dst: e.collective_compute(
                     "AllGather", ALU.bypass,
                     replica_groups=[[0, 1, 2, 3], [4, 5, 6, 7]],
                     ins=[src.ap().opt()], outs=[dst.ap().opt()]),
                 inc=sCC)
        v_cc = (sCC, sCC.n)
        refresh(["kt0", "kt1", "vs0", "vs1", "vs2", "qsb", "pbrow", "ptb0", "ptb1",
                 "PBc0", "PBc1", "PBc2", "PBc3"])

        for qb in range(NQB):
            dma("sync", qh_sb[:].rearrange("p (c t) -> p c t", c=NFC), qh_s.ap()[qb][:, :, :],
                waits=use("qsb") + ([(sSt, p0_store_n)] if qb == 0 else []), inc=sQl)
            dma("sync", ql_sb[:].rearrange("p (c t) -> p c t", c=NFC), ql_s.ap()[qb][:, :, :],
                inc=sQl)
            qtok = (sQl, sQl.n)
            for kt in range(NKT):
                par = kt % 2
                ksem = sKa if par == 0 else sKb
                dma("sync", kth[:, par*NFC*KT:(par+1)*NFC*KT].rearrange("p (c t) -> p c t", c=NFC),
                    kh_f.ap()[kt][:, :, :],
                    waits=use(f"kt{par}") + ([k_cc] if qb == 0 and kt < 2 else []), inc=ksem)
                dma("sync", ktl[:, par*NFC*KT:(par+1)*NFC*KT].rearrange("p (c t) -> p c t", c=NFC),
                    kl_f.ap()[kt][:, :, :], inc=ksem)
                pbk = PB[par]
                pairs = [(qh_sb[:, c*128:(c+1)*128], ql_sb[:, c*128:(c+1)*128],
                          kth[:, par*NFC*KT + c*KT: par*NFC*KT + (c+1)*KT],
                          ktl[:, par*NFC*KT + c*KT: par*NFC*KT + (c+1)*KT]) for c in range(NFC)]
                mm3(pbk[:, :KT], pairs,
                    waits_first=[(ksem, ksem.n), qtok] + use(f"PB{par}"), inc=sPE)
                vec(lambda e, pbk=pbk, kt=kt: e.tensor_copy(out=sc_sb[:, kt*KT:(kt+1)*KT],
                                                            in_=pbk[:, :KT]),
                    waits=[(sPE, sPE.n)], inc=sDV)
                setok(f"PB{par}", sDV)
                setok(f"kt{par}", sPE)
            setok("qsb", sPE)
            vec(lambda e: e.reduce_max(st128[:, 0:1], sc_sb[:, :S], axis=AXX))
            vec(lambda e: e.tensor_scalar_mul(st128[:, 1:2], st128[:, 0:1], -isq), inc=sDV)
            act(lambda e: e.activation(out=sc_sb[:, :S], in_=sc_sb[:, :S], func=AF.Exp,
                                       bias=st128[:, 1:2], scale=isq),
                waits=[(sDV, sDV.n)], inc=sAC)
            vec(lambda e: e.reduce_sum(st128[:, 2:3], sc_sb[:, :S], axis=AXX),
                waits=[(sAC, sAC.n)])
            vec(lambda e: e.reciprocal(st128[:, 3:4], st128[:, 2:3]))
            vec(lambda e: e.tensor_scalar_mul(sc_sb[:, :S], sc_sb[:, :S], st128[:, 3:4]),
                inc=sDV)
            vec(lambda e: e.tensor_copy(out=pbh_row[:, :S], in_=sc_sb[:, :S]),
                waits=use("pbrow"))
            vec(lambda e: e.tensor_sub(pbl_row[:, :S], sc_sb[:, :S], pbh_row[:, :S]),
                inc=sDV)
            prow_tok = (sDV, sDV.n)
            for slab in range(NSLAB):
                par = slab % 3
                vsem = (sVa, sVb, sCW[0])[par]
                for i in range(VSLAB):
                    dma("scalar", vsh[:, (par*VSLAB + i)*D:(par*VSLAB + i + 1)*D],
                        vh_f.ap()[slab*VSLAB + i][:, :],
                        waits=((use(f"vs{par}") if i == 0 else []) +
                               ([v_cc] if (qb == 0 and slab < 3 and i == 0) else [])),
                        inc=vsem)
                    dma("scalar", vsl[:, (par*VSLAB + i)*D:(par*VSLAB + i + 1)*D],
                        vl_f.ap()[slab*VSLAB + i][:, :], inc=vsem)
                vtok = (vsem, vsem.n)
                for i in range(VSLAB):
                    kc = slab * VSLAB + i
                    tp(PT_B[:, (2*i)*128:(2*i+1)*128], pbh_row[:, kc*128:(kc+1)*128],
                       identb[:],
                       waits=(([prow_tok] if (slab == 0 and i == 0) else []) +
                              (use("PT_B") if i == 0 else [])))
                    tp(PT_B[:, (2*i+1)*128:(2*i+2)*128], pbl_row[:, kc*128:(kc+1)*128],
                       identb[:], inc=(sPE if i == VSLAB - 1 else None))
                ppar = slab % 2
                vec(lambda e, ppar=ppar: e.tensor_copy(out=ptb[ppar][:, :VSLAB*256],
                                                       in_=PT_B[:, :VSLAB*256]),
                    waits=[(sPE, sPE.n)] + use(f"ptb{ppar}"), inc=sDV)
                setok("PT_B", sDV)
                pt_tok = (sDV, sDV.n)
                for i in range(VSLAB):
                    first = (slab == 0 and i == 0)
                    last = (slab == NSLAB - 1 and i == VSLAB - 1)
                    ph = ptb[ppar][:, (2*i)*128:(2*i+1)*128]
                    pl = ptb[ppar][:, (2*i+1)*128:(2*i+2)*128]
                    for n_ in range(NW5):
                        pbk = PB[2 + n_]
                        base = (par*VSLAB + i)*D + n_*W5
                        w0 = ([vtok, pt_tok] if n_ == 0 else []) + \
                             (use(f"PBc{n_}") if first else [])
                        mm(pbk[:, :W5], ph, vsh[:, base:base+W5],
                           start=first, stop=False, waits=w0)
                        mm(pbk[:, :W5], ph, vsl[:, base:base+W5],
                           start=False, stop=False)
                        mm(pbk[:, :W5], pl, vsh[:, base:base+W5],
                           start=False, stop=last,
                           inc=(sPE if (i == VSLAB - 1 and n_ == NW5 - 1) else None))
                setok(f"ptb{ppar}", sPE)
                setok(f"vs{par}", sPE)
            pe_last = (sPE, sPE.n)
            for n_ in range(NW5):
                vec(lambda e, n_=n_: e.tensor_copy(out=ctx32[:, n_*W5:(n_+1)*W5],
                                                   in_=PB[2 + n_][:, :W5]),
                    waits=([pe_last] + use("ctx32") if n_ == 0 else []), inc=sDV)
                setok(f"PBc{n_}", sDV)
            split32(pbh_row[:, :D], pbl_row[:, :D], ctx32[:, :], inc=sDV)
            setok("ctx32", sDV)
            cq_tok = (sDV, sDV.n)
            for fc in range(NFC):
                tp(PT_B[:, :128], pbh_row[:, fc*128:(fc+1)*128], identb[:],
                   waits=(([cq_tok] if fc == 0 else []) + use("PT_B")))
                tp(PT_B[:, 128:256], pbl_row[:, fc*128:(fc+1)*128], identb[:],
                   inc=sPE)
                vec(lambda e, fc=fc: e.tensor_copy(out=cth[:, fc*128:(fc+1)*128],
                                                   in_=PT_B[:, :128]),
                    waits=[(sPE, sPE.n)] + (use("cthl") if fc == 0 else []))
                vec(lambda e, fc=fc: e.tensor_copy(out=ctl[:, fc*128:(fc+1)*128],
                                                   in_=PT_B[:, 128:256]), inc=sDV)
                setok("PT_B", sDV)
            setok("pbrow", sPE)
            cx_tok = (sDV, sDV.n)
            dma("gpsimd", cxh_s.ap()[qb][:, :, :],
                cth[:].rearrange("p (c t) -> p c t", c=NFC), waits=[cx_tok], inc=sSt)
            dma("gpsimd", cxl_s.ap()[qb][:, :, :],
                ctl[:].rearrange("p (c t) -> p c t", c=NFC), inc=sSt)
            tok["cthl"] = [(sSt, sSt.n)]
        p1_store_n = sSt.n
        refresh(["xa", "zh", "fsb0", "fsb1", "fsb2", "fsb3", "fsb4", "fsb5",
                 "t32a", "t32b", "PB0", "PB1", "PB2", "PB3", "PB4", "PB5",
                 "z32sq", "hhl"])

        for tt in range(NTT):
            par = tt % 2
            xsem = sXa if par == 0 else sXb
            dma("sync", xa[:].rearrange("p (c t) -> p c t", c=NFC),
                x_own.ap()[:, :, tt*Tt:(tt+1)*Tt], waits=use("xa"), inc=xsem)
            xtok = (xsem, xsem.n)
            for qs in range(NQBT):
                qb = tt * NQBT + qs
                dma("sync", zh[:].rearrange("p (c t) -> p c t", c=NFC)[:, :, qs*128:(qs+1)*128],
                    cxh_s.ap()[qb][:, :, :],
                    waits=((use("zh") + [(sSt, p1_store_n)]) if qs == 0 else []), inc=sQl)
                dma("sync", zl[:].rearrange("p (c t) -> p c t", c=NFC)[:, :, qs*128:(qs+1)*128],
                    cxl_s.ap()[qb][:, :, :], inc=sQl)
            ctok = (sQl, sQl.n)
            ld_toks = {}
            for pf in range(min(3, NFC)):
                ld_toks[pf] = wload(woh_s.ap()[pf][:, :], wol_s.ap()[pf][:, :], pf)
            for fc in range(NFC):
                slot = fc % 4
                if fc + 3 < NFC:
                    ld_toks[fc + 3] = wload(woh_s.ap()[fc+3][:, :],
                                            wol_s.ap()[fc+3][:, :], (fc + 3) % 4)
                pbk = PB[fc % 2]
                wts = use(f"PB{fc % 2}") + [ld_toks[fc]] + ([ctok] if fc == 0 else [])
                pairs = [(fs_hi[:, slot*D + c*128: slot*D + c*128 + 128],
                          fs_lo[:, slot*D + c*128: slot*D + c*128 + 128],
                          zh[:, c*Tt:(c+1)*Tt], zl[:, c*Tt:(c+1)*Tt]) for c in range(NFC)]
                mm3(pbk[:, :Tt], pairs, waits_first=wts, inc=sPE)
                setok(f"fsb{slot}", sPE)
                setok("zh", sPE)
                vec(lambda e, pbk=pbk, fc=fc: e.tensor_scalar_add(t32a[:, :Tt], pbk[:, :Tt],
                                                                  bo_sb[:, fc:fc+1]),
                    waits=[(sPE, sPE.n), xtok] + use("t32a"))
                vec(lambda e, fc=fc: e.tensor_add(t32a[:, :Tt], t32a[:, :Tt],
                                                  xa[:, fc*Tt:(fc+1)*Tt]), inc=sDV)
                setok(f"PB{fc % 2}", sDV)
                dma("gpsimd", x1_s.ap()[tt][:, fc, :], t32a[:, :Tt],
                    waits=[(sDV, sDV.n)], inc=sSt)
                tok["t32a"] = [(sSt, sSt.n)]
            setok("xa", sDV)
        p2_store_n = sSt.n

        def ffn(xin_s, xin_ready_n, store_out):
            for tt in range(NTT):
                par = tt % 2
                xsem = sXa if par == 0 else sXb
                dma("sync", xa[:].rearrange("p (c t) -> p c t", c=NFC),
                    xin_s.ap()[tt][:, :, :],
                    waits=use("xa") + [(sSt, xin_ready_n)], inc=xsem)
                xtok = (xsem, xsem.n)
                ztok = layernorm(xa, Tt, xtok)
                for c in range(NFC):
                    mm(PB[5][:NE, :Tt], gwh[:, c*NE:(c+1)*NE], zh[:, c*Tt:(c+1)*Tt],
                       start=(c == 0), stop=False,
                       waits=(([ztok] + use("PB5")) if c == 0 else []))
                    mm(PB[5][:NE, :Tt], gwh[:, c*NE:(c+1)*NE], zl[:, c*Tt:(c+1)*Tt],
                       start=False, stop=False)
                    mm(PB[5][:NE, :Tt], gwl[:, c*NE:(c+1)*NE], zh[:, c*Tt:(c+1)*Tt],
                       start=False, stop=(c == NFC - 1), inc=(sPE if c == NFC - 1 else None))
                vec(lambda e: e.tensor_scalar(out=g32[:NE, :Tt], in0=PB[5][:NE, :Tt],
                                              scalar1=gbcol[:NE, :1], scalar2=None, op0=ALU.add),
                    waits=[(sPE, sPE.n)], inc=sDV)
                setok("PB5", sDV)
                g_tok = (sDV, sDV.n)
                for qs in range(NQBT):
                    tp(PT_F[:, :NE], g32[:NE, qs*128:(qs+1)*128], identf[:NE, :NE],
                       waits=[g_tok] + use("PT_F"), inc=sPE)
                    vec(lambda e, qs=qs: e.tensor_copy(out=gt_tm[:, qs*NE:(qs+1)*NE],
                                                       in_=PT_F[:, :NE]),
                        waits=[(sPE, sPE.n)], inc=sDV)
                    setok("PT_F", sDV)
                    gg = gt_tm[:, qs*NE:(qs+1)*NE]
                    vec(lambda e, gg=gg: e.reduce_max(st128[:, 0:1], gg, axis=AXX))
                    vec(lambda e: e.tensor_scalar_mul(st128[:, 1:2], st128[:, 0:1], -1.0),
                        inc=sDV)
                    act(lambda e, gg=gg: e.activation(out=gg, in_=gg, func=AF.Exp,
                                                      bias=st128[:, 1:2]),
                        waits=[(sDV, sDV.n)], inc=sAC)
                    vec(lambda e, gg=gg: e.reduce_sum(st128[:, 2:3], gg, axis=AXX),
                        waits=[(sAC, sAC.n)])
                    vec(lambda e: e.reciprocal(st128[:, 3:4], st128[:, 2:3]))
                    vec(lambda e, gg=gg: e.tensor_scalar_mul(gg, gg, st128[:, 3:4]))
                    vec(lambda e, gg=gg: e.tensor_add(gtmp[:, :NE], gg, gbb_sb[:, :NE]))
                    for ei in range(NE):
                        if ei > 0:
                            vec(lambda e, ei=ei: e.tensor_scalar(
                                    out=gtmp2[:, 0:ei], in0=gtmp[:, 0:ei],
                                    scalar1=gtmp[:, ei:ei+1], scalar2=None,
                                    op0=ALU.is_ge))
                        vec(lambda e, ei=ei: e.tensor_scalar(
                                out=gtmp2[:, ei:NE], in0=gtmp[:, ei:NE],
                                scalar1=gtmp[:, ei:ei+1], scalar2=None,
                                op0=ALU.is_gt))
                        vec(lambda e, ei=ei: e.reduce_sum(grk[:, ei:ei+1],
                                                          gtmp2[:, :NE], axis=AXX))
                    vec(lambda e: e.tensor_scalar(out=grk[:, :NE], in0=grk[:, :NE],
                                                  scalar1=2.5, scalar2=None, op0=ALU.is_lt))
                    vec(lambda e, gg=gg: e.tensor_mul(gg, gg, grk[:, :NE]))
                    vec(lambda e, gg=gg: e.reduce_sum(st128[:, 5:6], gg, axis=AXX))
                    vec(lambda e: e.reciprocal(st128[:, 6:7], st128[:, 5:6]))
                    vec(lambda e, gg=gg: e.tensor_scalar_mul(gg, gg, st128[:, 6:7]), inc=sDV)
                    tp(PT_F[:NE, :128], gg, identf[:, :],
                       waits=[(sDV, sDV.n)] + use("PT_F"), inc=sPE)
                    vec(lambda e, qs=qs: e.tensor_copy(out=gw_fm[:NE, qs*128:(qs+1)*128],
                                                       in_=PT_F[:NE, :128]),
                        waits=[(sPE, sPE.n)], inc=sDV)
                    setok("PT_F", sDV)
                gw_tok = (sDV, sDV.n)
                for ex in range(NES):
                    routed = ex > 0
                    if routed:
                        mm(PB[4][:, :Tt], oneh_sb[:NE, (ex-1)*128:ex*128], gw_fm[:NE, :Tt],
                           start=True, stop=True,
                           waits=[gw_tok] + use("PB4"), inc=sPE)
                        wb_tok = (sPE, sPE.n)
                    t1s, t2s, t3s = {}, {}, {}
                    for pf in range(min(2, NFC)):
                        t1s[pf] = wload(wfh_s.ap()[ex*3 + 0][pf][:, :],
                                        wfl_s.ap()[ex*3 + 0][pf][:, :], pf,
                                        queue="sync")
                        t2s[pf] = wload(wfh_s.ap()[ex*3 + 1][pf][:, :],
                                        wfl_s.ap()[ex*3 + 1][pf][:, :], 3 + pf,
                                        queue="scalar")
                    t3s[0] = wload(wfh_s.ap()[ex*3 + 2][0][:, :],
                                   wfl_s.ap()[ex*3 + 2][0][:, :], 6, queue="sync")
                    for fc in range(NFC):
                        s1 = fc % 3
                        s2 = 3 + fc % 3
                        if fc + 2 < NFC:
                            t1s[fc + 2] = wload(wfh_s.ap()[ex*3 + 0][fc+2][:, :],
                                                wfl_s.ap()[ex*3 + 0][fc+2][:, :],
                                                (fc + 2) % 3, queue="sync")
                            t2s[fc + 2] = wload(wfh_s.ap()[ex*3 + 1][fc+2][:, :],
                                                wfl_s.ap()[ex*3 + 1][fc+2][:, :],
                                                3 + (fc + 2) % 3, queue="scalar")
                        t1 = t1s[fc]
                        t2 = t2s[fc]
                        pb1 = PB[fc % 2]
                        pb2 = PB[2 + fc % 2]
                        pairs1 = [(fs_hi[:, s1*D + c*128: s1*D + (c+1)*128],
                                   fs_lo[:, s1*D + c*128: s1*D + (c+1)*128],
                                   zh[:, c*Tt:(c+1)*Tt], zl[:, c*Tt:(c+1)*Tt])
                                  for c in range(NFC)]
                        pairs2 = [(fs_hi[:, s2*D + c*128: s2*D + (c+1)*128],
                                   fs_lo[:, s2*D + c*128: s2*D + (c+1)*128],
                                   zh[:, c*Tt:(c+1)*Tt], zl[:, c*Tt:(c+1)*Tt])
                                  for c in range(NFC)]
                        mm3(pb1[:, :Tt], pairs1, waits_first=[t1] + use(f"PB{fc % 2}"), inc=sPE)
                        setok(f"fsb{s1}", sPE)
                        mm3(pb2[:, :Tt], pairs2,
                            waits_first=([t2] + use(f"PB{2 + fc % 2}") +
                                         ([wb_tok] if routed and fc == 0 else [])), inc=sPE)
                        setok(f"fsb{s2}", sPE)
                        pe_g = (sPE, sPE.n)
                        act(lambda e, pb1=pb1: e.activation(out=t32a[:, :Tt], in_=pb1[:, :Tt],
                                                            func=AF.Silu),
                            waits=[pe_g] + use("t32a"), inc=sAC)
                        setok(f"PB{fc % 2}", sAC)
                        vec(lambda e, pb2=pb2: e.tensor_mul(t32b[:, :Tt], t32a[:, :Tt],
                                                            pb2[:, :Tt]),
                            waits=[(sAC, sAC.n)] + use("t32b"), inc=sDV)
                        setok(f"PB{2 + fc % 2}", sDV)
                        setok("t32a", sDV)
                        if routed:
                            vec(lambda e: e.tensor_mul(t32b[:, :Tt], t32b[:, :Tt], PB[4][:, :Tt]))
                        split32(hh[:, fc*Tt:(fc+1)*Tt], hl[:, fc*Tt:(fc+1)*Tt],
                                t32b[:, :Tt], waits=use("hhl"), inc=sDV)
                        setok("t32b", sDV)
                    h_tok = (sDV, sDV.n)
                    if routed:
                        setok("PB4", sDV)
                    if NFC > 1:
                        t3s[1] = wload(wfh_s.ap()[ex*3 + 2][1][:, :],
                                       wfl_s.ap()[ex*3 + 2][1][:, :], 0, queue="sync")
                    for fc in range(NFC):
                        s3 = 6 if fc % 2 == 0 else 0
                        t3 = t3s[fc]
                        pb3 = PB[fc % 2]
                        pairs3 = [(fs_hi[:, s3*D + c*128: s3*D + (c+1)*128],
                                   fs_lo[:, s3*D + c*128: s3*D + (c+1)*128],
                                   hh[:, c*Tt:(c+1)*Tt], hl[:, c*Tt:(c+1)*Tt])
                                  for c in range(NFC)]
                        mm3(pb3[:, :Tt], pairs3,
                            waits_first=[t3, h_tok] + use(f"PB{fc % 2}"), inc=sPE)
                        setok(f"fsb{s3}", sPE)
                        if fc + 2 < NFC:
                            t3s[fc + 2] = wload(wfh_s.ap()[ex*3 + 2][fc+2][:, :],
                                                wfl_s.ap()[ex*3 + 2][fc+2][:, :],
                                                s3, queue="sync")
                        if fc == NFC - 1:
                            setok("hhl", sPE)
                        if ex == 0:
                            vec(lambda e, pb3=pb3, fc=fc: e.tensor_copy(
                                    out=acc[:, fc*Tt:(fc+1)*Tt], in_=pb3[:, :Tt]),
                                waits=[(sPE, sPE.n)] + (use("acc") if fc == 0 else []),
                                inc=sDV)
                        else:
                            vec(lambda e, pb3=pb3, fc=fc: e.tensor_add(
                                    acc[:, fc*Tt:(fc+1)*Tt], acc[:, fc*Tt:(fc+1)*Tt],
                                    pb3[:, :Tt]),
                                waits=[(sPE, sPE.n)], inc=sDV)
                        setok(f"PB{fc % 2}", sDV)
                acc_tok = (sDV, sDV.n)
                for fc in range(NFC):
                    vec(lambda e, fc=fc: e.tensor_add(t32b[:, :Tt], acc[:, fc*Tt:(fc+1)*Tt],
                                                      xa[:, fc*Tt:(fc+1)*Tt]),
                        waits=([acc_tok] if fc == 0 else []) + use("t32b"), inc=sDV)
                    store_out(tt, fc)
                    tok["t32b"] = [(sSt, sSt.n)]
                setok("acc", sDV)
                setok("xa", sDV)

        def store_x2(tt, fc):
            dma("gpsimd", x2_s.ap()[tt][:, fc, :], t32b[:, :Tt],
                waits=[(sDV, sDV.n)], inc=sSt)
        ffn(x1_s, p2_store_n, store_x2)
        p3_store_n = sSt.n

        for tt in range(NTT):
            par = tt % 2
            xsem = sXa if par == 0 else sXb
            dma("sync", xa[:].rearrange("p (c t) -> p c t", c=NFC),
                x2_s.ap()[tt][:, :, :], waits=use("xa") + [(sSt, p3_store_n)], inc=xsem)
            xtok = (xsem, xsem.n)
            ztok = layernorm(xa, Tt, xtok)
            for ts in range(Tt // 128):
                tc = tt * (Tt // 128) + ts
                for fc in range(NFC):
                    tp(PT_B[:, :128], zh[:, fc*Tt + ts*128: fc*Tt + ts*128 + 128], identb[:],
                       waits=(([ztok] if fc == 0 else []) + use("PT_B")))
                    tp(PT_B[:, 128:256], zl[:, fc*Tt + ts*128: fc*Tt + ts*128 + 128], identb[:],
                       inc=sPE)
                    vec(lambda e, fc=fc: e.tensor_copy(out=cth[:, fc*128:(fc+1)*128],
                                                       in_=PT_B[:, :128]),
                        waits=[(sPE, sPE.n)] + (use("cthl") if fc == 0 else []))
                    vec(lambda e, fc=fc: e.tensor_copy(out=ctl[:, fc*128:(fc+1)*128],
                                                       in_=PT_B[:, 128:256]), inc=sDV)
                    setok("PT_B", sDV)
                    tp(PT_F[:, :128], xa[:, fc*Tt + ts*128: fc*Tt + ts*128 + 128], identf[:],
                       waits=use("PT_F"), inc=sPE)
                    vec(lambda e, fc=fc: e.tensor_copy(out=ctx32[:, fc*128:(fc+1)*128],
                                                       in_=PT_F[:, :128]),
                        waits=[(sPE, sPE.n)] + (use("ctx32") if fc == 0 else []), inc=sDV)
                    setok("PT_F", sDV)
                zt = (sDV, sDV.n)
                dma("gpsimd", z3h_loc.ap()[tc][:, :], cth[:, :], waits=[zt], inc=sSt)
                dma("gpsimd", z3l_loc.ap()[tc][:, :], ctl[:, :], inc=sSt)
                dma("gpsimd", s_acc.ap()[tc][:, :], ctx32[:, :], inc=sSt)
                tok["cthl"] = [(sSt, sSt.n)]
                tok["ctx32"] = [(sSt, sSt.n)]
            setok("xa", sPE)
        p3b_store_n = sSt.n

        gp(lambda e: e.memset(oblk[0][:, :], 0.0), waits=use("oblk0"), inc=sGP)
        dma("gpsimd", s_acc.ap()[T // 128][0:BS, :], oblk[0][:, :],
            waits=[(sGP, sGP.n)], inc=sSt)
        if 2 * BS <= 128:
            dma("gpsimd", s_acc.ap()[T // 128][BS:2*BS, :], oblk[0][:, :], inc=sSt)
        tok["oblk0"] = [(sSt, sSt.n)]
        dump_n = sSt.n

        p.wait("gpsimd", sSt, dump_n)
        for (src, dst) in ((z3h_loc, z3h_f), (z3l_loc, z3l_f)):
            p.op("gpsimd",
                 lambda e, src=src, dst=dst: e.collective_compute(
                     "AllGather", ALU.bypass,
                     replica_groups=[[0, 1, 2, 3], [4, 5, 6, 7]],
                     ins=[src.ap().opt()], outs=[dst.ap().opt()]),
                 inc=sCC)
        z3_cc = (sCC, sCC.n)
        refresh(["zblk0", "zblk1", "zblk2", "wblk0", "wblk1", "oblk0", "oblk1", "scat",
                 "wblk320", "wblk321", "PB0", "PB1"])

        z3h_flat = z3h_f.ap().rearrange("a p d -> (a p) d")
        z3l_flat = z3l_f.ap().rearrange("a p d -> (a p) d")
        sacc_flat = s_acc.ap().rearrange("a p d -> (a p) d")
        for i in range(3):
            gp(lambda e, i=i: e.memset(zblk_h[i][:, :16], 0.0),
               waits=use(f"zblk{i}") if i == 0 else ())
            gp(lambda e, i=i: e.memset(zblk_l[i][:, :16], 0.0),
               inc=(sGP if i == 2 else None))
        pad_tok = (sGP, sGP.n)

        def conv_gather(j, par, first):
            p.op("gpsimd",
                 lambda e, j=j, par=par: e.indirect_dma_start(
                     out=zblk_h[par][0:BS, 16:16+D], out_offset=None, in_=z3h_flat,
                     in_offset=bass.IndirectOffsetOnAxis(idx_sb[:, 2*j:2*j+1], 0)),
                 waits=(([z3_cc, pad_tok, (sSt, dump_n), setup_ld] if first else []) +
                        use(f"zblk{par}")),
                 inc=sHg[par], n=16)
            p.op("gpsimd",
                 lambda e, j=j, par=par: e.indirect_dma_start(
                     out=zblk_l[par][0:BS, 16:16+D], out_offset=None, in_=z3l_flat,
                     in_offset=bass.IndirectOffsetOnAxis(idx_sb[:, 2*j:2*j+1], 0)),
                 waits=use(f"zblk{par}") if not first else [],
                 inc=sLg[par], n=16)
            g = [(sHg[par], sHg[par].n), (sLg[par], sLg[par].n)]
            dma("sync", zblk_h[par][BS:2*BS, 15:15+D], zblk_h[par][0:BS, 16:16+D],
                waits=g, inc=sG[par])
            dma("sync", zblk_l[par][BS:2*BS, 15:15+D], zblk_l[par][0:BS, 16:16+D],
                inc=sG[par])
            return (sG[par], sG[par].n)

        g_toks = {}
        for pf in range(min(2, M)):
            g_toks[pf] = conv_gather(pf, pf % 3, pf == 0)
        KP = KS // 2
        for j in range(M):
            par = j % 3
            if j + 2 < M:
                g_toks[j + 2] = conv_gather(j + 2, (j + 2) % 3, False)
            g_tok = g_toks[j]
            wpar = j % 2
            semw = sCW[wpar]
            dma("sync", wblk32[wpar][:, :], bvals_h.ap()[j][:, :],
                waits=use(f"wblk32{wpar}"), inc=semw)
            vec(lambda e, wpar=wpar: e.tensor_copy(out=wblk_h[wpar][:, :],
                                                   in_=wblk32[wpar][:, :]),
                waits=[(semw, semw.n)] + use(f"wblk{wpar}"))
            vec(lambda e, wpar=wpar: e.tensor_sub(wblk_l[wpar][:, :], wblk32[wpar][:, :],
                                                  wblk_h[wpar][:, :]), inc=sDV)
            setok(f"wblk32{wpar}", sDV)
            w_tok = (sDV, sDV.n)
            for ts in range(NW5):
                pbk = PB[ts % 2]
                for kp in range(KP):
                    wts = ([g_tok, w_tok] + use(f"PB{ts % 2}")) if kp == 0 else ()
                    mm(pbk[:BS, :W5], wblk_h[wpar][:, kp*BS:(kp+1)*BS],
                       zblk_h[par][:, ts*W5 + 2*kp + 1: ts*W5 + 2*kp + 1 + W5],
                       start=(kp == 0), stop=False, waits=wts)
                    mm(pbk[:BS, :W5], wblk_h[wpar][:, kp*BS:(kp+1)*BS],
                       zblk_l[par][:, ts*W5 + 2*kp + 1: ts*W5 + 2*kp + 1 + W5],
                       start=False, stop=False)
                    mm(pbk[:BS, :W5], wblk_l[wpar][:, kp*BS:(kp+1)*BS],
                       zblk_h[par][:, ts*W5 + 2*kp + 1: ts*W5 + 2*kp + 1 + W5],
                       start=False, stop=(kp == KP - 1), inc=(sPE if kp == KP - 1 else None))
                vec(lambda e, pbk=pbk, ts=ts, j=j: e.tensor_copy(
                        out=oblk[j % 2][:, ts*W5:(ts+1)*W5], in_=pbk[:BS, :W5]),
                    waits=[(sPE, sPE.n)] + (use(f"oblk{j % 2}") if ts == 0 else []),
                    inc=sDV)
                setok(f"PB{ts % 2}", sDV)
            setok(f"zblk{par}", sPE)
            setok(f"wblk{wpar}", sPE)
            o_tok = (sDV, sDV.n)
            p.op("gpsimd",
                 lambda e, j=j: e.indirect_dma_start(
                     out=sacc_flat,
                     out_offset=bass.IndirectOffsetOnAxis(idx_sb[:, 2*j+1:2*j+2], 0),
                     in_=oblk[j % 2][:, :], compute_op=ALU.add, in_offset=None),
                 waits=[o_tok] + use("scat"), inc=sSt, n=16)
            tok["scat"] = [(sSt, sSt.n)]
            tok[f"oblk{j % 2}"] = [(sSt, sSt.n)]
        p4_store_n = sSt.n
        refresh(["xa", "PB0", "PB1", "PT_F", "acc", "z32sq", "t32a", "t32b", "zh",
                 "hhl", "cthl", "ctx32", "PB2", "PB3", "PB4", "PB5"])

        for tt in range(NTT):
            for ts in range(Tt // 128):
                tc = tt * (Tt // 128) + ts
                par = tc % 2
                xsem = sXa if par == 0 else sXb
                dma("sync", xa[:, :D], s_acc.ap()[tc][:, :],
                    waits=use("xa") + ([(sSt, p4_store_n)] if tc == 0 else []), inc=xsem)
                stok = (xsem, xsem.n)
                for fc in range(NFC):
                    tp(PT_F[:, :128], xa[:, fc*128:(fc+1)*128], identf[:],
                       waits=(([stok] if fc == 0 else []) + use("PT_F")), inc=sPE)
                    vec(lambda e, fc=fc, ts=ts: e.tensor_copy(
                            out=acc[:, fc*Tt + ts*128: fc*Tt + ts*128 + 128],
                            in_=PT_F[:, :128]),
                        waits=[(sPE, sPE.n)] + (use("acc") if (ts == 0 and fc == 0) else []),
                        inc=sDV)
                    setok("PT_F", sDV)
                setok("xa", sPE)
            x3tok = (sDV, sDV.n)
            dma("gpsimd", x3_s.ap()[tt][:, :, :],
                acc[:].rearrange("p (c t) -> p c t", c=NFC), waits=[x3tok], inc=sSt)
            tok["acc"] = [(sSt, sSt.n)]
        p4b_store_n = sSt.n

        def store_y(tt, fc):
            dma("gpsimd", y_h.ap()[:, fc, tt*Tt:(tt+1)*Tt], t32b[:, :Tt],
                waits=[(sDV, sDV.n)], inc=sSt)
        ffn(x3_s, p4b_store_n, store_y)

        p.wait("gpsimd", sSt, sSt.n)

    with nc.Block() as block:
        block.sync(lambda e: program(P("sync", e)))
        block.tensor(lambda e: program(P("tensor", e)))
        block.vector(lambda e: program(P("vector", e)))
        block.scalar(lambda e: program(P("scalar", e)))
        block.gpsimd(lambda e: program(P("gpsimd", e)))

    es.close()
    return nc



def _fm(W):
    Din = W.shape[0]
    return np.ascontiguousarray(W.reshape(Din // 128, 128, -1).transpose(1, 0, 2))


def _stripes(W):
    Din, Dout = W.shape
    nfi, nfo = Din // 128, Dout // 128
    return np.ascontiguousarray(
        W.reshape(nfi, 128, nfo, 128).transpose(2, 1, 0, 3).reshape(nfo, 128, Din))


def stage_inputs(cfg, inputs):
    D, S, B, NE, T = cfg.D, cfg.S, cfg.B, cfg.NE, cfg.T
    NFC, BS, KS, M = cfg.NFC, cfg.BS, cfg.KS, cfg.M
    x = np.asarray(inputs["L0_attn_a_hidden_states"], np.float32)
    w_qkv = np.asarray(inputs["L0_attn_a_w_qkv"], np.float32)
    b_qkv = np.asarray(inputs["L0_attn_a_b_qkv"], np.float32)
    w_out = np.asarray(inputs["L0_attn_a_w_out"], np.float32)
    b_out = np.asarray(inputs["L0_attn_a_b_out"], np.float32)
    bvals = np.asarray(inputs["L0_attn_b_block_values"], np.float32)
    cols = np.asarray(inputs["L0_attn_b_cols"], np.int64)
    rows = np.asarray(inputs["L0_attn_b_rows"], np.int64)
    W1 = np.asarray(inputs["L0_ffn_W1"], np.float32)
    Wg = np.asarray(inputs["L0_ffn_Wg"], np.float32)
    W2 = np.asarray(inputs["L0_ffn_W2"], np.float32)
    Ws1 = np.asarray(inputs["L0_ffn_Ws1"], np.float32)
    Wsg = np.asarray(inputs["L0_ffn_Wsg"], np.float32)
    Ws2 = np.asarray(inputs["L0_ffn_Ws2"], np.float32)
    gate_w = np.asarray(inputs["L0_ffn_gate_w"], np.float32)
    gate_b = np.asarray(inputs["L0_ffn_gate_b"], np.float32)
    biases = np.asarray(inputs["L0_ffn_biases"], np.float32)

    wq = _stripes(w_qkv[0:D].T)
    wk = _stripes(w_qkv[D:2*D].T)
    wv = _stripes(w_qkv[2*D:3*D].T)
    wo = _stripes(w_out.T)
    bq = np.ascontiguousarray(b_qkv[0:D].reshape(NFC, 128).T)
    bk = np.ascontiguousarray(b_qkv[D:2*D].reshape(NFC, 128).T)
    bo = np.ascontiguousarray(b_out.reshape(NFC, 128).T)
    bv = np.ascontiguousarray(b_qkv[2*D:3*D].reshape(NFC, 128).T)
    mats = [Ws1, Wsg, Ws2]
    for e in range(NE):
        mats += [W1[e], Wg[e], W2[e]]
    wf = np.ascontiguousarray(np.stack([_stripes(m) for m in mats], axis=0))
    gw = np.ascontiguousarray(gate_w.reshape(NFC, 128, NE).transpose(1, 0, 2))
    gb = np.ascontiguousarray(gate_b.reshape(1, NE))
    gbias = np.ascontiguousarray(biases.reshape(1, NE))
    identb = np.eye(128, dtype=ml_dtypes.bfloat16)
    identf = np.eye(128, dtype=np.float32)
    oneh = np.zeros((NE, NE * 128), np.float32)
    for e in range(NE):
        oneh[e, e*128:(e+1)*128] = 1.0

    nqrt = T // BS
    in_maps = []
    for c in range(8):
        b, qq = divmod(c, 4)
        own = [i for i in range(len(rows)) if qq * nqrt <= rows[i] < (qq + 1) * nqrt]
        assert len(own) <= M
        bv_own = np.zeros((M, 128, (KS // 2) * BS), np.float32)
        colrow = np.zeros((BS, M, 2), np.int32)
        for j in range(M):
            if j < len(own):
                i = own[j]
                arr = bvals[i].transpose(1, 2, 0)
                bv_own[j][0:BS] = arr[:, 0::2, :].reshape(BS, (KS // 2) * BS)
                bv_own[j][BS:2*BS] = arr[:, 1::2, :].reshape(BS, (KS // 2) * BS)
                colrow[:, j, 0] = cols[i] * BS + np.arange(BS)
                colrow[:, j, 1] = (rows[i] - qq * nqrt) * BS + np.arange(BS)
            else:
                colrow[:, j, 0] = np.arange(BS)
                colrow[:, j, 1] = T + np.arange(BS)
        x_own = np.ascontiguousarray(
            x[b].T[:, qq*T:(qq+1)*T].reshape(NFC, 128, T).transpose(1, 0, 2))
        in_maps.append({
            "x_own": x_own, "wq": wq, "wk": wk, "wv": wv, "wo": wo,
            "bq": bq, "bk": bk, "bo": bo, "bv": bv,
            "wf": wf, "gw": gw, "gb": gb, "gbias": gbias,
            "bvals": bv_own, "colrow": colrow,
            "identb": identb, "identf": identf, "oneh": oneh,
        })
    return in_maps


def assemble(cfg, results):
    D, S, B, T, NFC = cfg.D, cfg.S, cfg.B, cfg.T, cfg.NFC
    out = np.empty((B, S, D), np.float32)
    for c in range(8):
        b, qq = divmod(c, 4)
        y = np.asarray(results[c]["y"]).reshape(128, NFC, T)
        out[b, qq*T:(qq+1)*T, :] = y.transpose(1, 0, 2).reshape(D, T).T
    return out


def kernel(**inputs):
    x = np.asarray(inputs["L0_attn_a_hidden_states"])
    B, S, D = x.shape
    NE = int(np.asarray(inputs["L0_ffn_gate_w"]).shape[1])
    NBLK, BS, _, KS = np.asarray(inputs["L0_attn_b_block_values"]).shape
    rows = np.asarray(inputs["L0_attn_b_rows"], np.int64)
    T = B * S // 8
    nqrt = T // BS
    mx = max(int(np.sum((rows >= q * nqrt) & (rows < (q + 1) * nqrt))) for q in range(4))
    M = max(mx, 2)
    cfg = Cfg(D=D, S=S, B=B, NE=NE, NBLK=NBLK, BS=BS, KS=KS, M=M)
    nc = build_nc(cfg)
    in_maps = stage_inputs(cfg, inputs)
    res = run_bass_kernel_spmd(nc, in_maps, core_ids=list(range(8)))
    return assemble(cfg, res.results)


# version 11
# speedup vs baseline: 1.0071x; 1.0071x over previous
import math
from contextlib import ExitStack
from functools import reduce
import operator
import numpy as np
import ml_dtypes
import concourse.bass as bass
import concourse.mybir as mybir
from concourse.bass_utils import run_bass_kernel_spmd

F32 = mybir.dt.float32
BF16 = mybir.dt.bfloat16
I32 = mybir.dt.int32
AF = mybir.ActivationFunctionType
ALU = mybir.AluOpType
AXX = mybir.AxisListType.X


class Cfg:
    def __init__(self, D=2048, S=8192, B=2, NE=7, NBLK=128, BS=64, KS=16, M=64):
        self.D, self.S, self.B, self.NE = D, S, B, NE
        self.NBLK, self.BS, self.KS, self.M = NBLK, BS, KS, M
        self.NFC = D // 128
        self.T = B * S // 8
        assert self.T == S // 4
        self.Tt = min(512, self.T)
        self.NTT = self.T // self.Tt
        self.NKC = S // 128
        self.NQB = self.T // 128
        self.KT = min(256, S)
        self.NKT = S // self.KT
        self.VSLAB = 2
        self.NSLAB = self.NKC // self.VSLAB
        self.eps = 1e-5
        self.isq = 1.0 / math.sqrt(D)


class Sem:
    __slots__ = ("h", "n", "name")
    def __init__(self, h, name):
        self.h, self.n, self.name = h, 0, name


class P:
    def __init__(self, en, eng):
        self.en, self.e = en, eng

    def op(self, en, fn, waits=(), inc=None, n=1, drain=False):
        r = None
        if en == self.en:
            for (s, v) in waits:
                self.e.wait_ge(s.h, v)
            r = fn(self.e)
            if inc is not None:
                r.then_inc(inc.h, n)
            if drain:
                self.e.drain()
        if inc is not None:
            inc.n += n
        return r

    def wait(self, en, s, v):
        if en == self.en:
            self.e.wait_ge(s.h, v)


class Arena:
    def __init__(self, nc, name):
        self.nc, self.name = nc, name
        self.tensors, self.size = [], 0

    def at(self, off, name, shape, dtype):
        per = reduce(operator.mul, list(shape[1:]), 1) * mybir.dt.size(dtype)
        assert off % 32 == 0, name
        self.size = max(self.size, off + per)
        h = self.nc.virtual_sbuf_tensor(name, list(shape), dtype)
        self.tensors.append((h, off))
        return h

    def finalize(self, es):
        handle = es.enter_context(
            self.nc.sbuf_tensor(self.name, [128, self.size], mybir.dt.uint8))
        base = self.nc.lookup_mloc(handle).addr
        for h, off in self.tensors:
            self.nc._tensor(h.name, h.shape, h.dtype, type="SB")
            mloc = self.nc.lookup_mloc(h)
            mloc.allocated = not self.nc.target_bir_lowering
            mloc.addr = base + off


def _align(x, a=32):
    return (x + a - 1) // a * a


def build_nc(cfg: Cfg):
    nc = bass.Bass("TRN2", target_bir_lowering=False)
    D, S, B, NE = cfg.D, cfg.S, cfg.B, cfg.NE
    NFC, T, Tt, NTT = cfg.NFC, cfg.T, cfg.Tt, cfg.NTT
    NKC, NQB, KT, NKT = cfg.NKC, cfg.NQB, cfg.KT, cfg.NKT
    VSLAB, NSLAB = cfg.VSLAB, cfg.NSLAB
    BS, KS, M = cfg.BS, cfg.KS, cfg.M
    NES = NE + 1
    NQBT = Tt // 128
    NKTT = Tt // KT
    W5 = min(512, D)
    NW5 = D // W5
    assert NW5 <= 4
    eps, isq = cfg.eps, cfg.isq

    x_own = nc.dram_tensor("x_own", [128, NFC, T], F32, kind="ExternalInput")
    wq_h = nc.dram_tensor("wq", [NFC, 128, D], F32, kind="ExternalInput")
    wk_h = nc.dram_tensor("wk", [NFC, 128, D], F32, kind="ExternalInput")
    wv_h = nc.dram_tensor("wv", [NFC, 128, D], F32, kind="ExternalInput")
    wo_h = nc.dram_tensor("wo", [NFC, 128, D], F32, kind="ExternalInput")
    bq_h = nc.dram_tensor("bq", [128, NFC], F32, kind="ExternalInput")
    bk_h = nc.dram_tensor("bk", [128, NFC], F32, kind="ExternalInput")
    bo_h = nc.dram_tensor("bo", [128, NFC], F32, kind="ExternalInput")
    bv_h = nc.dram_tensor("bv", [128, NFC], F32, kind="ExternalInput")
    wf_h = nc.dram_tensor("wf", [NES * 3, NFC, 128, D], F32, kind="ExternalInput")
    gw_h = nc.dram_tensor("gw", [128, NFC, NE], F32, kind="ExternalInput")
    gb_h = nc.dram_tensor("gb", [1, NE], F32, kind="ExternalInput")
    gbias_h = nc.dram_tensor("gbias", [1, NE], F32, kind="ExternalInput")
    bvals_h = nc.dram_tensor("bvals", [M, 128, (KS // 2) * BS], F32, kind="ExternalInput")
    colrow_h = nc.dram_tensor("colrow", [BS, M, 2], I32, kind="ExternalInput")
    oneh_h = nc.dram_tensor("oneh", [NE, NE * 128], F32, kind="ExternalInput")
    identb_h = nc.dram_tensor("identb", [128, 128], BF16, kind="ExternalInput")
    identf_h = nc.dram_tensor("identf", [128, 128], F32, kind="ExternalInput")

    y_h = nc.dram_tensor("y", [128, NFC, T], F32, kind="ExternalOutput")

    wqh_s = nc.dram_tensor("wqh_s", [NFC, 128, D], BF16)
    wql_s = nc.dram_tensor("wql_s", [NFC, 128, D], BF16)
    wkh_s = nc.dram_tensor("wkh_s", [NFC, 128, D], BF16)
    wkl_s = nc.dram_tensor("wkl_s", [NFC, 128, D], BF16)
    wvh_s = nc.dram_tensor("wvh_s", [NFC, 128, D], BF16)
    wvl_s = nc.dram_tensor("wvl_s", [NFC, 128, D], BF16)
    woh_s = nc.dram_tensor("woh_s", [NFC, 128, D], BF16)
    wol_s = nc.dram_tensor("wol_s", [NFC, 128, D], BF16)
    wfh_s = nc.dram_tensor("wfh_s", [NES * 3, NFC, 128, D], BF16)
    wfl_s = nc.dram_tensor("wfl_s", [NES * 3, NFC, 128, D], BF16)
    kh_loc = nc.dram_tensor("kh_loc", [NKT // 4, 128, NFC, KT], BF16)
    kl_loc = nc.dram_tensor("kl_loc", [NKT // 4, 128, NFC, KT], BF16)
    vh_loc = nc.dram_tensor("vh_loc", [NKC // 4, 128, D], BF16)
    vl_loc = nc.dram_tensor("vl_loc", [NKC // 4, 128, D], BF16)
    kh_f = nc.dram_tensor("kh_f", [NKT, 128, NFC, KT], BF16)
    kl_f = nc.dram_tensor("kl_f", [NKT, 128, NFC, KT], BF16)
    vh_f = nc.dram_tensor("vh_f", [NKC, 128, D], BF16)
    vl_f = nc.dram_tensor("vl_f", [NKC, 128, D], BF16)
    qh_s = nc.dram_tensor("qh_s", [NQB, 128, NFC, 128], BF16)
    ql_s = nc.dram_tensor("ql_s", [NQB, 128, NFC, 128], BF16)
    cxh_s = nc.dram_tensor("cxh_s", [NQB, 128, NFC, 128], BF16)
    cxl_s = nc.dram_tensor("cxl_s", [NQB, 128, NFC, 128], BF16)
    x1_s = nc.dram_tensor("x1_s", [NTT, 128, NFC, Tt], F32)
    x2_s = nc.dram_tensor("x2_s", [NTT, 128, NFC, Tt], F32)
    x3_s = nc.dram_tensor("x3_s", [NTT, 128, NFC, Tt], F32)
    z3h_loc = nc.dram_tensor("z3h_loc", [T // 128, 128, D], BF16)
    z3l_loc = nc.dram_tensor("z3l_loc", [T // 128, 128, D], BF16)
    z3h_f = nc.dram_tensor("z3h_f", [S // 128, 128, D], BF16)
    z3l_f = nc.dram_tensor("z3l_f", [S // 128, 128, D], BF16)
    s_acc = nc.dram_tensor("s_acc", [T // 128 + 1, 128, D], F32)

    es = ExitStack()
    def sb(name, shape, dtype):
        return es.enter_context(nc.sbuf_tensor(name, shape, dtype))
    def psum(name, shape, dtype=F32):
        return es.enter_context(nc.psum_tensor(name, shape, dtype))

    identb = sb("c_identb", [128, 128], BF16)
    identf = sb("c_identf", [128, 128], F32)
    ones1 = sb("c_ones1", [128, 1], F32)
    onesr = sb("c_onesr", [1, 128], F32)
    bq_sb = sb("c_bq", [128, NFC], F32)
    bk_sb = sb("c_bk", [128, NFC], F32)
    bo_sb = sb("c_bo", [128, NFC], F32)
    bv_sb = sb("c_bv", [128, NFC], F32)
    gw32 = sb("c_gw32", [128, NFC * NE], F32)
    gwh = sb("c_gwh", [128, NFC * NE], BF16)
    gwl = sb("c_gwl", [128, NFC * NE], BF16)
    gb_sb = sb("c_gb", [1, NE], F32)
    gbias_sb = sb("c_gbias", [1, NE], F32)
    gbb_sb = sb("c_gbb", [128, NE], F32)
    gbcol = sb("c_gbcol", [NE, 1], F32)
    idx_sb = sb("c_idx", [BS, 2 * M], I32)
    st128 = sb("c_st128", [128, 8], F32)
    g32 = sb("c_g32", [NE, Tt], F32)
    gt_tm = sb("c_gt_tm", [128, NQBT * NE], F32)
    grk = sb("c_grk", [128, NE], F32)
    gtmp = sb("c_gtmp", [128, NE], F32)
    gtmp2 = sb("c_gtmp2", [128, NE], F32)
    gw_fm = sb("c_gwfm", [NE, Tt], F32)
    oneh_sb = sb("c_oneh", [NE, NE * 128], F32)

    A = Arena(nc, "arena")
    o = 0
    r1 = o
    q = r1
    xa = A.at(q, "r_xa", [128, NFC * Tt], F32); q += NFC * Tt * 4
    zh = A.at(q, "r_zh", [128, NFC * Tt], BF16); q += NFC * Tt * 2
    zl = A.at(q, "r_zl", [128, NFC * Tt], BF16); q += NFC * Tt * 2
    z32 = A.at(q, "r_z32", [128, Tt], F32); q += Tt * 4
    t32a = A.at(q, "r_t32a", [128, Tt], F32); q += Tt * 4
    t32b = A.at(q, "r_t32b", [128, Tt], F32); q += Tt * 4
    stats = A.at(q, "r_stats", [1, 3 * 512], F32); q += 3 * 512 * 4
    hh = A.at(q, "r_hh", [128, NFC * Tt], BF16); q += NFC * Tt * 2
    hl = A.at(q, "r_hl", [128, NFC * Tt], BF16); q += NFC * Tt * 2
    r1_end_ln = q
    q = r1
    kth = A.at(q, "r_kth", [128, 2 * NFC * KT], BF16); q += 2 * NFC * KT * 2
    ktl = A.at(q, "r_ktl", [128, 2 * NFC * KT], BF16); q += 2 * NFC * KT * 2
    sc_sb = A.at(q, "r_sc", [128, S], F32); q += S * 4
    qh_sb = A.at(q, "r_qh", [128, NFC * 128], BF16); q += NFC * 128 * 2
    ql_sb = A.at(q, "r_ql", [128, NFC * 128], BF16); q += NFC * 128 * 2
    pbh_row = A.at(q, "r_pbh_row", [128, S], BF16); q += S * 2
    pbl_row = A.at(q, "r_pbl_row", [128, S], BF16); q += S * 2
    ptb = []
    for i in range(2):
        ptb.append(A.at(q, f"r_ptb{i}", [128, VSLAB * 256], BF16)); q += VSLAB * 512
    r1_end_att = q
    q = r1
    zpad = _align((16 + D) * 2)
    WW = (KS // 2) * BS
    zblk_h, zblk_l, wblk32, wblk_h, wblk_l = [], [], [], [], []
    for i in range(3):
        zblk_h.append(A.at(q, f"r_zblk_h{i}", [128, 16 + D], BF16)); q += zpad
    for i in range(3):
        zblk_l.append(A.at(q, f"r_zblk_l{i}", [128, 16 + D], BF16)); q += zpad
    for i in range(2):
        wblk32.append(A.at(q, f"r_wblk32{i}", [128, WW], F32)); q += WW * 4
    for i in range(2):
        wblk_h.append(A.at(q, f"r_wblk_h{i}", [128, WW], BF16)); q += WW * 2
    for i in range(2):
        wblk_l.append(A.at(q, f"r_wblk_l{i}", [128, WW], BF16)); q += WW * 2
    oblk = []
    for i in range(2):
        oblk.append(A.at(q, f"r_oblk{i}", [BS, D], F32)); q += D * 4
    r1_end_cv = q
    q = r1
    pw32, pwh, pwl = [], [], []
    for i in range(2):
        pw32.append(A.at(q, f"r_pw32_{i}", [128, 2 * D], F32)); q += 2 * D * 4
    for i in range(2):
        pwh.append(A.at(q, f"r_pwh_{i}", [128, 2 * D], BF16)); q += 2 * D * 2
    for i in range(2):
        pwl.append(A.at(q, f"r_pwl_{i}", [128, 2 * D], BF16)); q += 2 * D * 2
    r1_end_pw = q
    o = _align(max(r1_end_ln, r1_end_att, r1_end_cv, r1_end_pw))

    r2 = o
    q = r2
    fs_hi = A.at(q, "r_fs_hi", [128, 6 * D], BF16); q += 6 * D * 2
    fs_lo = A.at(q, "r_fs_lo", [128, 6 * D], BF16); q += 6 * D * 2
    r2_end_w = q
    q = r2
    vsh = A.at(q, "r_vsh", [128, 3 * VSLAB * D], BF16); q += 3 * VSLAB * D * 2
    vsl = A.at(q, "r_vsl", [128, 3 * VSLAB * D], BF16); q += 3 * VSLAB * D * 2
    r2_end_v = q
    o = _align(max(r2_end_w, r2_end_v))

    r3 = o
    acc = A.at(r3, "r_acc", [128, NFC * Tt], F32)
    ctx32 = A.at(r3, "r_ctx32", [128, D], F32)
    cth = A.at(_align(r3 + D * 4), "r_cth", [128, D], BF16)
    ctl = A.at(_align(r3 + D * 4 + D * 2), "r_ctl", [128, D], BF16)
    o = _align(max(r3 + NFC * Tt * 4, r3 + D * 4 + 2 * D * 2))

    A.finalize(es)

    PB = [psum(f"p_b{i}", [128, 512], F32) for i in range(6)]
    PT_B = psum("p_trb", [128, 1024], BF16)
    PT_F = psum("p_trf", [128, 512], F32)

    sems = []
    def mks(name):
        s = Sem(es.enter_context(nc.semaphore(name)), name)
        sems.append(s)
        return s

    sPE = mks("sPE"); sDV = mks("sDV"); sAC = mks("sAC"); sGP = mks("sGP")
    sCC = mks("sCC")
    sSt = mks("sSt"); sLd = mks("sLd")
    sXa = mks("sXa"); sXb = mks("sXb")
    sF = [mks(f"sF{i}") for i in range(4)]
    sQl = mks("sQl")
    sKa = mks("sKa"); sKb = mks("sKb")
    sVa = mks("sVa"); sVb = mks("sVb")
    sCW = [mks(f"sCW{i}") for i in range(2)]
    sPA = mks("sPA"); sPB = mks("sPB")
    sG = [mks(f"sG{i}") for i in range(3)]
    sHg = [mks(f"sHg{i}") for i in range(3)]
    sLg = [mks(f"sLg{i}") for i in range(3)]

    def program(p: P):
        for s in sems:
            s.n = 0
        tok = {}

        def use(*names):
            out = []
            for r in names:
                out.extend(tok.get(r, []))
            return out

        def setok(res, sem):
            tok[res] = [(sem, sem.n)]

        def refresh(names):
            v = [(sm, sm.n) for sm in sems if sm.n > 0]
            for r in names:
                tok[r] = list(v)

        def dma(en, out, in_, waits=(), inc=None):
            return p.op(en, lambda e: e.dma_start(out=out, in_=in_),
                        waits=waits, inc=inc, n=16)

        def mm(out, lhsT, rhs, start, stop, waits=(), inc=None):
            return p.op("tensor",
                        lambda e: e.matmul(out, lhsT, rhs, start=start, stop=stop),
                        waits=waits, inc=inc)

        def tp(out, in_, ident, waits=(), inc=None):
            return p.op("tensor", lambda e: e.transpose(out, in_, ident),
                        waits=waits, inc=inc)

        def vec(fn, waits=(), inc=None):
            return p.op("vector", fn, waits=waits, inc=inc, drain=True)

        def act(fn, waits=(), inc=None):
            return p.op("scalar", fn, waits=waits, inc=inc, drain=True)

        def gp(fn, waits=(), inc=None):
            return p.op("gpsimd", fn, waits=waits, inc=inc, drain=True)

        def mm3(out, pairs, waits_first=(), inc=None):
            n = len(pairs)
            for i, (lh, ll, rh, rl) in enumerate(pairs):
                w = waits_first if i == 0 else ()
                mm(out, lh, rh, start=(i == 0), stop=False, waits=w)
                mm(out, lh, rl, start=False, stop=False)
                mm(out, ll, rh, start=False, stop=(i == n - 1),
                   inc=(inc if i == n - 1 else None))

        def split32(dst_h, dst_l, src32, scr32=None, waits=(), inc=None):
            vec(lambda e: e.tensor_copy(out=dst_h, in_=src32), waits=waits)
            vec(lambda e: e.tensor_sub(dst_l, src32, dst_h), inc=inc)

        slot_sems = [sF[0], sF[1], sF[2], sF[3], sKa, sKb]

        def wload(hi_src, lo_src, slot, waits_extra=(), queue="sync"):
            semf = slot_sems[slot]
            dma(queue, fs_hi[:, slot*D:(slot+1)*D], hi_src,
                waits=use(f"fsb{slot}") + list(waits_extra), inc=semf)
            dma(queue, fs_lo[:, slot*D:(slot+1)*D], lo_src, inc=semf)
            return (semf, semf.n)

        dma("sync", identb[:], identb_h.ap(), inc=sQl)
        dma("sync", identf[:], identf_h.ap(), inc=sQl)
        dma("sync", bq_sb[:], bq_h.ap(), inc=sQl)
        dma("sync", bk_sb[:], bk_h.ap(), inc=sQl)
        dma("sync", bo_sb[:], bo_h.ap(), inc=sQl)
        dma("sync", bv_sb[:], bv_h.ap(), inc=sQl)
        dma("sync", gw32[:].rearrange("p (c n) -> p c n", c=NFC), gw_h.ap(), inc=sQl)
        dma("sync", gb_sb[:], gb_h.ap(), inc=sQl)
        dma("sync", gbias_sb[:], gbias_h.ap(), inc=sQl)
        dma("sync", idx_sb[:].rearrange("p (m two) -> p m two", two=2), colrow_h.ap(), inc=sQl)
        dma("sync", oneh_sb[:], oneh_h.ap(), inc=sQl)
        setup_ld = (sQl, sQl.n)
        gp(lambda e: e.memset(ones1[:], 1.0))
        gp(lambda e: e.memset(onesr[:], 1.0), inc=sGP)
        ones_tok = (sGP, sGP.n)
        split32(gwh[:], gwl[:], gw32[:], waits=[setup_ld], inc=sDV)
        mm(PT_F[:, :NE], onesr[:], gbias_sb[:, :], start=True, stop=True,
           waits=[setup_ld, ones_tok] + use("PT_F"), inc=sPE)
        vec(lambda e: e.tensor_copy(out=gbb_sb[:, :NE], in_=PT_F[:, :NE]),
            waits=[(sPE, sPE.n)], inc=sDV)
        setok("PT_F", sDV)
        tp(PT_F[:NE, :1], gb_sb[:, :NE], identf[:1, :1], waits=use("PT_F"), inc=sPE)
        vec(lambda e: e.tensor_copy(out=gbcol[:NE, :1], in_=PT_F[:NE, :1]),
            waits=[(sPE, sPE.n)], inc=sDV)
        setok("PT_F", sDV)

        pre_srcs = []
        PWB = 2 if NFC % 2 == 0 else 1
        def _pair(apw, fc):
            if PWB == 1:
                return apw[fc][:, :]
            return apw[fc:fc+PWB].rearrange("c p d -> p c d")
        def _sb2(t):
            if PWB == 1:
                return t[:, :D]
            return t[:].rearrange("p (c d) -> p c d", c=PWB)
        for (srcw, dh, dl) in ((wq_h, wqh_s, wql_s), (wk_h, wkh_s, wkl_s),
                               (wv_h, wvh_s, wvl_s), (wo_h, woh_s, wol_s)):
            for fc in range(0, NFC, PWB):
                pre_srcs.append((_pair(srcw.ap(), fc), _pair(dh.ap(), fc),
                                 _pair(dl.ap(), fc)))
        for mi in range(NES * 3):
            for fc in range(0, NFC, PWB):
                pre_srcs.append((_pair(wf_h.ap()[mi], fc), _pair(wfh_s.ap()[mi], fc),
                                 _pair(wfl_s.ap()[mi], fc)))
        for i, (src, dh, dl) in enumerate(pre_srcs):
            slot = i % 2
            qd = "sync" if i % 2 == 0 else "scalar"
            lsem = sXa if i % 2 == 0 else sXb
            stsem = sPA if i % 2 == 0 else sPB
            st_gate = [(stsem, stsem.n)] if stsem.n > 0 else []
            dma(qd, _sb2(pw32[slot]), src, waits=use(f"pw32_{slot}"), inc=lsem)
            ld_tok = (lsem, lsem.n)
            act(lambda e, slot=slot: e.activation(out=pwh[slot][:, :],
                                                  in_=pw32[slot][:, :], func=AF.Copy),
                waits=[ld_tok] + st_gate, inc=sAC)
            cast_tok = (sAC, sAC.n)
            if i % 2 == 0:
                vec(lambda e, slot=slot: e.tensor_sub(pwl[slot][:, :], pw32[slot][:, :],
                                                      pwh[slot][:, :]),
                    waits=[cast_tok], inc=sDV)
                sp_tok = (sDV, sDV.n)
            else:
                gp(lambda e, slot=slot: e.tensor_sub(pwl[slot][:, :], pw32[slot][:, :],
                                                     pwh[slot][:, :]),
                   waits=[cast_tok], inc=sGP)
                sp_tok = (sGP, sGP.n)
            tok[f"pw32_{slot}"] = [sp_tok]
            dma(qd, dh, _sb2(pwh[slot]), waits=[sp_tok], inc=stsem)
            dma(qd, dl, _sb2(pwl[slot]), inc=stsem)
        pre_done = (sPA, sPA.n)
        pre_done_b = (sPB, sPB.n)
        refresh(["xa"])

        def layernorm(xs, W, xs_tok):
            sq_prev = None
            for c in range(NFC):
                act(lambda e, c=c: e.activation(out=z32[:, :W], in_=xs[:, c*W:(c+1)*W],
                                                func=AF.Square),
                    waits=(([xs_tok] + use("z32sq")) if c == 0 else [(sPE, sq_prev)]),
                    inc=sAC)
                mm(PB[4][:1, :W], ones1[:], xs[:, c*W:(c+1)*W],
                   start=(c == 0), stop=(c == NFC - 1),
                   waits=(([xs_tok, ones_tok] + use("PB4", "PB5")) if c == 0 else []))
                mm(PB[5][:1, :W], ones1[:], z32[:, :W],
                   start=(c == 0), stop=(c == NFC - 1),
                   waits=[(sAC, sAC.n)], inc=sPE)
                sq_prev = sPE.n
            setok("z32sq", sPE)
            pe_stats = (sPE, sPE.n)
            vec(lambda e: e.tensor_scalar_mul(stats[:1, 0:W], PB[4][:1, :W], 1.0 / D),
                waits=[pe_stats])
            vec(lambda e: e.tensor_scalar_mul(stats[:1, 512:512+W], PB[5][:1, :W], 1.0 / D))
            vec(lambda e: e.tensor_tensor(stats[:1, 1024:1024+W], stats[:1, 0:W],
                                          stats[:1, 0:W], ALU.mult))
            vec(lambda e: e.tensor_sub(stats[:1, 512:512+W], stats[:1, 512:512+W],
                                       stats[:1, 1024:1024+W]))
            vec(lambda e: e.tensor_scalar_add(stats[:1, 512:512+W], stats[:1, 512:512+W], eps),
                inc=sDV)
            setok("PB4", sDV); setok("PB5", sDV)
            act(lambda e: e.activation(out=stats[:1, 1024:1024+W], in_=stats[:1, 512:512+W],
                                       func=AF.Sqrt),
                waits=[(sDV, sDV.n)], inc=sAC)
            vec(lambda e: e.reciprocal(stats[:1, 512:512+W], stats[:1, 1024:1024+W]),
                waits=[(sAC, sAC.n)])
            vec(lambda e: e.tensor_tensor(stats[:1, 1024:1024+W], stats[:1, 0:W],
                                          stats[:1, 512:512+W], ALU.mult))
            vec(lambda e: e.tensor_scalar_mul(stats[:1, 1024:1024+W], stats[:1, 1024:1024+W],
                                              -1.0), inc=sDV)
            assert W <= 512
            mm(PB[2][:, :W], onesr[:], stats[:1, 512:512+W], start=True, stop=True,
               waits=[(sDV, sDV.n)] + use("PB2", "PB3"))
            mm(PB[3][:, :W], onesr[:], stats[:1, 1024:1024+W], start=True, stop=True, inc=sPE)
            pe_bc = (sPE, sPE.n)
            for c in range(NFC):
                vec(lambda e, c=c: e.tensor_tensor(z32[:, :W], xs[:, c*W:(c+1)*W],
                                                   PB[2][:, :W], ALU.mult),
                    waits=(([pe_bc] + use("zh")) if c == 0 else []))
                vec(lambda e, c=c: e.tensor_add(z32[:, :W], z32[:, :W], PB[3][:, :W]))
                vec(lambda e, c=c: e.tensor_copy(out=zh[:, c*W:(c+1)*W], in_=z32[:, :W]))
                vec(lambda e, c=c: e.tensor_sub(zl[:, c*W:(c+1)*W], z32[:, :W],
                                                zh[:, c*W:(c+1)*W]),
                    inc=(sDV if c == NFC - 1 else None))
            setok("PB2", sDV); setok("PB3", sDV); setok("z32sq", sDV)
            return (sDV, sDV.n)

        def p0_all():
            for tt in range(NTT):
                par = tt % 2
                xsem = sXa if par == 0 else sXb
                dma("sync", xa[:].rearrange("p (c t) -> p c t", c=NFC),
                    x_own.ap()[:, :, tt*Tt:(tt+1)*Tt], waits=use("xa"), inc=xsem)
                xtok = (xsem, xsem.n)
                ztok = layernorm(xa, Tt, xtok)
                setok("xa", sDV)
                for kind in ("q", "k", "v"):
                    w_hi, w_lo = {"q": (wqh_s, wql_s), "k": (wkh_s, wkl_s),
                                  "v": (wvh_s, wvl_s)}[kind]
                    b_sb = {"q": bq_sb, "k": bk_sb, "v": bv_sb}[kind]
                    ld_toks = {}
                    for pf in range(min(3, NFC)):
                        extra = ([pre_done, pre_done_b]
                                 if (tt == 0 and kind == "q" and pf == 0) else [])
                        ld_toks[pf] = wload(w_hi.ap()[pf][:, :], w_lo.ap()[pf][:, :],
                                            pf, waits_extra=extra)
                    for fc in range(NFC):
                        slot = fc % 4
                        if fc + 3 < NFC:
                            ld_toks[fc + 3] = wload(w_hi.ap()[fc+3][:, :],
                                                    w_lo.ap()[fc+3][:, :], (fc + 3) % 4)
                        pbk = PB[fc % 2]
                        wts = use(f"PB{fc % 2}") + [ld_toks[fc]] + ([ztok] if fc == 0 else [])
                        pairs = [(fs_hi[:, slot*D + c*128: slot*D + c*128 + 128],
                                  fs_lo[:, slot*D + c*128: slot*D + c*128 + 128],
                                  zh[:, c*Tt:(c+1)*Tt], zl[:, c*Tt:(c+1)*Tt])
                                 for c in range(NFC)]
                        mm3(pbk[:, :Tt], pairs, waits_first=wts, inc=sPE)
                        setok(f"fsb{slot}", sPE)
                        vec(lambda e, pbk=pbk, fc=fc, b_sb=b_sb:
                            e.tensor_scalar_add(t32a[:, :Tt], pbk[:, :Tt], b_sb[:, fc:fc+1]),
                            waits=[(sPE, sPE.n)] + use("t32a", "hhl"))
                        split32(hh[:, fc*Tt:(fc+1)*Tt], hl[:, fc*Tt:(fc+1)*Tt],
                                t32a[:, :Tt], inc=sDV)
                        setok(f"PB{fc % 2}", sDV)
                        setok("t32a", sDV)
                    done = (sDV, sDV.n)
                    if kind == "q":
                        for qs in range(NQBT):
                            qb = tt * NQBT + qs
                            dma("gpsimd", qh_s.ap()[qb][:, :, :],
                                hh[:].rearrange("p (c t) -> p c t", c=NFC)[:, :, qs*128:(qs+1)*128],
                                waits=([done] if qs == 0 else []), inc=sSt)
                            dma("gpsimd", ql_s.ap()[qb][:, :, :],
                                hl[:].rearrange("p (c t) -> p c t", c=NFC)[:, :, qs*128:(qs+1)*128],
                                inc=sSt)
                        tok["hhl"] = [(sSt, sSt.n)]
                    elif kind == "k":
                        for ks_ in range(NKTT):
                            kt = tt * NKTT + ks_
                            dma("gpsimd", kh_loc.ap()[kt][:, :, :],
                                hh[:].rearrange("p (c t) -> p c t", c=NFC)[:, :, ks_*KT:(ks_+1)*KT],
                                waits=([done] if ks_ == 0 else []), inc=sSt)
                            dma("gpsimd", kl_loc.ap()[kt][:, :, :],
                                hl[:].rearrange("p (c t) -> p c t", c=NFC)[:, :, ks_*KT:(ks_+1)*KT],
                                inc=sSt)
                        tok["hhl"] = [(sSt, sSt.n)]
                        if tt == NTT - 1:
                            p.wait("gpsimd", sSt, sSt.n)
                            for (csrc, cdst) in ((kh_loc, kh_f), (kl_loc, kl_f)):
                                p.op("gpsimd",
                                     lambda e, csrc=csrc, cdst=cdst: e.collective_compute(
                                         "AllGather", ALU.bypass,
                                         replica_groups=[[0, 1, 2, 3], [4, 5, 6, 7]],
                                         ins=[csrc.ap().opt()], outs=[cdst.ap().opt()]),
                                     inc=sCC)
                    else:
                        for ts in range(NQBT):
                            kc = tt * NQBT + ts
                            for fc in range(NFC):
                                tp(PT_B[:, :128],
                                   hh[:, fc*Tt + ts*128: fc*Tt + ts*128 + 128], identb[:],
                                   waits=(([done] if (ts == 0 and fc == 0) else []) +
                                          use("PT_B")))
                                tp(PT_B[:, 128:256],
                                   hl[:, fc*Tt + ts*128: fc*Tt + ts*128 + 128], identb[:],
                                   inc=sPE)
                                vec(lambda e, fc=fc: e.tensor_copy(
                                        out=cth[:, fc*128:(fc+1)*128], in_=PT_B[:, :128]),
                                    waits=[(sPE, sPE.n)] + (use("cthl") if fc == 0 else []))
                                vec(lambda e, fc=fc: e.tensor_copy(
                                        out=ctl[:, fc*128:(fc+1)*128], in_=PT_B[:, 128:256]),
                                    inc=sDV)
                                setok("PT_B", sDV)
                            vtok = (sDV, sDV.n)
                            dma("gpsimd", vh_loc.ap()[kc][:, :], cth[:, :],
                                waits=[vtok], inc=sSt)
                            dma("gpsimd", vl_loc.ap()[kc][:, :], ctl[:, :], inc=sSt)
                            tok["cthl"] = [(sSt, sSt.n)]
                        tok["hhl"] = [(sPE, sPE.n)]

        p0_all()
        p0_store_n = sSt.n

        k_cc = (sCC, sCC.n)
        p.wait("gpsimd", sSt, p0_store_n)
        for (src, dst) in ((vh_loc, vh_f), (vl_loc, vl_f)):
            p.op("gpsimd",
                 lambda e, src=src, dst=dst: e.collective_compute(
                     "AllGather", ALU.bypass,
                     replica_groups=[[0, 1, 2, 3], [4, 5, 6, 7]],
                     ins=[src.ap().opt()], outs=[dst.ap().opt()]),
                 inc=sCC)
        v_cc = (sCC, sCC.n)
        refresh(["kt0", "kt1", "vs0", "vs1", "vs2", "qsb", "pbrow", "ptb0", "ptb1",
                 "PBc0", "PBc1", "PBc2", "PBc3"])

        for qb in range(NQB):
            dma("sync", qh_sb[:].rearrange("p (c t) -> p c t", c=NFC), qh_s.ap()[qb][:, :, :],
                waits=use("qsb") + ([(sSt, p0_store_n)] if qb == 0 else []), inc=sQl)
            dma("sync", ql_sb[:].rearrange("p (c t) -> p c t", c=NFC), ql_s.ap()[qb][:, :, :],
                inc=sQl)
            qtok = (sQl, sQl.n)
            for kt in range(NKT):
                par = kt % 2
                ksem = sKa if par == 0 else sKb
                dma("sync", kth[:, par*NFC*KT:(par+1)*NFC*KT].rearrange("p (c t) -> p c t", c=NFC),
                    kh_f.ap()[kt][:, :, :],
                    waits=use(f"kt{par}") + ([k_cc] if qb == 0 and kt < 2 else []), inc=ksem)
                dma("sync", ktl[:, par*NFC*KT:(par+1)*NFC*KT].rearrange("p (c t) -> p c t", c=NFC),
                    kl_f.ap()[kt][:, :, :], inc=ksem)
                pbk = PB[par]
                pairs = [(qh_sb[:, c*128:(c+1)*128], ql_sb[:, c*128:(c+1)*128],
                          kth[:, par*NFC*KT + c*KT: par*NFC*KT + (c+1)*KT],
                          ktl[:, par*NFC*KT + c*KT: par*NFC*KT + (c+1)*KT]) for c in range(NFC)]
                mm3(pbk[:, :KT], pairs,
                    waits_first=[(ksem, ksem.n), qtok] + use(f"PB{par}"), inc=sPE)
                act(lambda e, pbk=pbk, kt=kt: e.activation(
                        out=sc_sb[:, kt*KT:(kt+1)*KT], in_=pbk[:, :KT], func=AF.Copy),
                    waits=[(sPE, sPE.n)] + (use("scrow") if kt == 0 else []), inc=sAC)
                setok(f"PB{par}", sAC)
                setok(f"kt{par}", sPE)
            setok("qsb", sPE)
            vec(lambda e: e.reduce_max(st128[:, 0:1], sc_sb[:, :S], axis=AXX),
                waits=[(sAC, sAC.n)])
            vec(lambda e: e.tensor_scalar_mul(st128[:, 1:2], st128[:, 0:1], -isq), inc=sDV)
            act(lambda e: e.activation(out=sc_sb[:, :S], in_=sc_sb[:, :S], func=AF.Exp,
                                       bias=st128[:, 1:2], scale=isq),
                waits=[(sDV, sDV.n)], inc=sAC)
            vec(lambda e: e.reduce_sum(st128[:, 2:3], sc_sb[:, :S], axis=AXX),
                waits=[(sAC, sAC.n)])
            vec(lambda e: e.reciprocal(st128[:, 3:4], st128[:, 2:3]))
            vec(lambda e: e.tensor_scalar_mul(sc_sb[:, :S], sc_sb[:, :S], st128[:, 3:4]),
                inc=sDV)
            vec(lambda e: e.tensor_copy(out=pbh_row[:, :S], in_=sc_sb[:, :S]),
                waits=use("pbrow"))
            vec(lambda e: e.tensor_sub(pbl_row[:, :S], sc_sb[:, :S], pbh_row[:, :S]),
                inc=sDV)
            prow_tok = (sDV, sDV.n)
            setok("scrow", sDV)
            for slab in range(NSLAB):
                par = slab % 3
                vsem = (sVa, sVb, sCW[0])[par]
                for i in range(VSLAB):
                    dma("scalar", vsh[:, (par*VSLAB + i)*D:(par*VSLAB + i + 1)*D],
                        vh_f.ap()[slab*VSLAB + i][:, :],
                        waits=((use(f"vs{par}") if i == 0 else []) +
                               ([v_cc] if (qb == 0 and slab < 3 and i == 0) else [])),
                        inc=vsem)
                    dma("scalar", vsl[:, (par*VSLAB + i)*D:(par*VSLAB + i + 1)*D],
                        vl_f.ap()[slab*VSLAB + i][:, :], inc=vsem)
                vtok = (vsem, vsem.n)
                for i in range(VSLAB):
                    kc = slab * VSLAB + i
                    tp(PT_B[:, (2*i)*128:(2*i+1)*128], pbh_row[:, kc*128:(kc+1)*128],
                       identb[:],
                       waits=(([prow_tok] if (slab == 0 and i == 0) else []) +
                              (use("PT_B") if i == 0 else [])))
                    tp(PT_B[:, (2*i+1)*128:(2*i+2)*128], pbl_row[:, kc*128:(kc+1)*128],
                       identb[:], inc=(sPE if i == VSLAB - 1 else None))
                ppar = slab % 2
                vec(lambda e, ppar=ppar: e.tensor_copy(out=ptb[ppar][:, :VSLAB*256],
                                                       in_=PT_B[:, :VSLAB*256]),
                    waits=[(sPE, sPE.n)] + use(f"ptb{ppar}"), inc=sDV)
                setok("PT_B", sDV)
                pt_tok = (sDV, sDV.n)
                for i in range(VSLAB):
                    first = (slab == 0 and i == 0)
                    last = (slab == NSLAB - 1 and i == VSLAB - 1)
                    ph = ptb[ppar][:, (2*i)*128:(2*i+1)*128]
                    pl = ptb[ppar][:, (2*i+1)*128:(2*i+2)*128]
                    for n_ in range(NW5):
                        pbk = PB[2 + n_]
                        base = (par*VSLAB + i)*D + n_*W5
                        w0 = ([vtok, pt_tok] if n_ == 0 else []) + \
                             (use(f"PBc{n_}") if first else [])
                        mm(pbk[:, :W5], ph, vsh[:, base:base+W5],
                           start=first, stop=False, waits=w0)
                        mm(pbk[:, :W5], ph, vsl[:, base:base+W5],
                           start=False, stop=False)
                        mm(pbk[:, :W5], pl, vsh[:, base:base+W5],
                           start=False, stop=last,
                           inc=(sPE if (i == VSLAB - 1 and n_ == NW5 - 1) else None))
                setok(f"ptb{ppar}", sPE)
                setok(f"vs{par}", sPE)
            pe_last = (sPE, sPE.n)
            for n_ in range(NW5):
                vec(lambda e, n_=n_: e.tensor_copy(out=ctx32[:, n_*W5:(n_+1)*W5],
                                                   in_=PB[2 + n_][:, :W5]),
                    waits=([pe_last] + use("ctx32") if n_ == 0 else []), inc=sDV)
                setok(f"PBc{n_}", sDV)
            split32(pbh_row[:, :D], pbl_row[:, :D], ctx32[:, :], inc=sDV)
            setok("ctx32", sDV)
            cq_tok = (sDV, sDV.n)
            for fc in range(NFC):
                tp(PT_B[:, :128], pbh_row[:, fc*128:(fc+1)*128], identb[:],
                   waits=(([cq_tok] if fc == 0 else []) + use("PT_B")))
                tp(PT_B[:, 128:256], pbl_row[:, fc*128:(fc+1)*128], identb[:],
                   inc=sPE)
                vec(lambda e, fc=fc: e.tensor_copy(out=cth[:, fc*128:(fc+1)*128],
                                                   in_=PT_B[:, :128]),
                    waits=[(sPE, sPE.n)] + (use("cthl") if fc == 0 else []))
                vec(lambda e, fc=fc: e.tensor_copy(out=ctl[:, fc*128:(fc+1)*128],
                                                   in_=PT_B[:, 128:256]), inc=sDV)
                setok("PT_B", sDV)
            setok("pbrow", sPE)
            cx_tok = (sDV, sDV.n)
            dma("gpsimd", cxh_s.ap()[qb][:, :, :],
                cth[:].rearrange("p (c t) -> p c t", c=NFC), waits=[cx_tok], inc=sSt)
            dma("gpsimd", cxl_s.ap()[qb][:, :, :],
                ctl[:].rearrange("p (c t) -> p c t", c=NFC), inc=sSt)
            tok["cthl"] = [(sSt, sSt.n)]
        p1_store_n = sSt.n
        refresh(["xa", "zh", "fsb0", "fsb1", "fsb2", "fsb3", "fsb4", "fsb5",
                 "t32a", "t32b", "PB0", "PB1", "PB2", "PB3", "PB4", "PB5",
                 "z32sq", "hhl"])

        for tt in range(NTT):
            par = tt % 2
            xsem = sXa if par == 0 else sXb
            dma("sync", xa[:].rearrange("p (c t) -> p c t", c=NFC),
                x_own.ap()[:, :, tt*Tt:(tt+1)*Tt], waits=use("xa"), inc=xsem)
            xtok = (xsem, xsem.n)
            for qs in range(NQBT):
                qb = tt * NQBT + qs
                dma("sync", zh[:].rearrange("p (c t) -> p c t", c=NFC)[:, :, qs*128:(qs+1)*128],
                    cxh_s.ap()[qb][:, :, :],
                    waits=((use("zh") + [(sSt, p1_store_n)]) if qs == 0 else []), inc=sQl)
                dma("sync", zl[:].rearrange("p (c t) -> p c t", c=NFC)[:, :, qs*128:(qs+1)*128],
                    cxl_s.ap()[qb][:, :, :], inc=sQl)
            ctok = (sQl, sQl.n)
            ld_toks = {}
            for pf in range(min(3, NFC)):
                ld_toks[pf] = wload(woh_s.ap()[pf][:, :], wol_s.ap()[pf][:, :], pf)
            for fc in range(NFC):
                slot = fc % 4
                if fc + 3 < NFC:
                    ld_toks[fc + 3] = wload(woh_s.ap()[fc+3][:, :],
                                            wol_s.ap()[fc+3][:, :], (fc + 3) % 4)
                pbk = PB[fc % 2]
                wts = use(f"PB{fc % 2}") + [ld_toks[fc]] + ([ctok] if fc == 0 else [])
                pairs = [(fs_hi[:, slot*D + c*128: slot*D + c*128 + 128],
                          fs_lo[:, slot*D + c*128: slot*D + c*128 + 128],
                          zh[:, c*Tt:(c+1)*Tt], zl[:, c*Tt:(c+1)*Tt]) for c in range(NFC)]
                mm3(pbk[:, :Tt], pairs, waits_first=wts, inc=sPE)
                setok(f"fsb{slot}", sPE)
                setok("zh", sPE)
                vec(lambda e, pbk=pbk, fc=fc: e.tensor_scalar_add(t32a[:, :Tt], pbk[:, :Tt],
                                                                  bo_sb[:, fc:fc+1]),
                    waits=[(sPE, sPE.n), xtok] + use("t32a"))
                vec(lambda e, fc=fc: e.tensor_add(t32a[:, :Tt], t32a[:, :Tt],
                                                  xa[:, fc*Tt:(fc+1)*Tt]), inc=sDV)
                setok(f"PB{fc % 2}", sDV)
                dma("gpsimd", x1_s.ap()[tt][:, fc, :], t32a[:, :Tt],
                    waits=[(sDV, sDV.n)], inc=sSt)
                tok["t32a"] = [(sSt, sSt.n)]
            setok("xa", sDV)
        p2_store_n = sSt.n

        def ffn(xin_s, xin_ready_n, store_out):
            for tt in range(NTT):
                par = tt % 2
                xsem = sXa if par == 0 else sXb
                dma("sync", xa[:].rearrange("p (c t) -> p c t", c=NFC),
                    xin_s.ap()[tt][:, :, :],
                    waits=use("xa") + [(sSt, xin_ready_n)], inc=xsem)
                xtok = (xsem, xsem.n)
                ztok = layernorm(xa, Tt, xtok)
                for c in range(NFC):
                    mm(PB[5][:NE, :Tt], gwh[:, c*NE:(c+1)*NE], zh[:, c*Tt:(c+1)*Tt],
                       start=(c == 0), stop=False,
                       waits=(([ztok] + use("PB5")) if c == 0 else []))
                    mm(PB[5][:NE, :Tt], gwh[:, c*NE:(c+1)*NE], zl[:, c*Tt:(c+1)*Tt],
                       start=False, stop=False)
                    mm(PB[5][:NE, :Tt], gwl[:, c*NE:(c+1)*NE], zh[:, c*Tt:(c+1)*Tt],
                       start=False, stop=(c == NFC - 1), inc=(sPE if c == NFC - 1 else None))
                vec(lambda e: e.tensor_scalar(out=g32[:NE, :Tt], in0=PB[5][:NE, :Tt],
                                              scalar1=gbcol[:NE, :1], scalar2=None, op0=ALU.add),
                    waits=[(sPE, sPE.n)], inc=sDV)
                setok("PB5", sDV)
                g_tok = (sDV, sDV.n)
                for qs in range(NQBT):
                    tp(PT_F[:, :NE], g32[:NE, qs*128:(qs+1)*128], identf[:NE, :NE],
                       waits=[g_tok] + use("PT_F"), inc=sPE)
                    vec(lambda e, qs=qs: e.tensor_copy(out=gt_tm[:, qs*NE:(qs+1)*NE],
                                                       in_=PT_F[:, :NE]),
                        waits=[(sPE, sPE.n)], inc=sDV)
                    setok("PT_F", sDV)
                    gg = gt_tm[:, qs*NE:(qs+1)*NE]
                    vec(lambda e, gg=gg: e.reduce_max(st128[:, 0:1], gg, axis=AXX))
                    vec(lambda e: e.tensor_scalar_mul(st128[:, 1:2], st128[:, 0:1], -1.0),
                        inc=sDV)
                    act(lambda e, gg=gg: e.activation(out=gg, in_=gg, func=AF.Exp,
                                                      bias=st128[:, 1:2]),
                        waits=[(sDV, sDV.n)], inc=sAC)
                    vec(lambda e, gg=gg: e.reduce_sum(st128[:, 2:3], gg, axis=AXX),
                        waits=[(sAC, sAC.n)])
                    vec(lambda e: e.reciprocal(st128[:, 3:4], st128[:, 2:3]))
                    vec(lambda e, gg=gg: e.tensor_scalar_mul(gg, gg, st128[:, 3:4]))
                    vec(lambda e, gg=gg: e.tensor_add(gtmp[:, :NE], gg, gbb_sb[:, :NE]))
                    for ei in range(NE):
                        if ei > 0:
                            vec(lambda e, ei=ei: e.tensor_scalar(
                                    out=gtmp2[:, 0:ei], in0=gtmp[:, 0:ei],
                                    scalar1=gtmp[:, ei:ei+1], scalar2=None,
                                    op0=ALU.is_ge))
                        vec(lambda e, ei=ei: e.tensor_scalar(
                                out=gtmp2[:, ei:NE], in0=gtmp[:, ei:NE],
                                scalar1=gtmp[:, ei:ei+1], scalar2=None,
                                op0=ALU.is_gt))
                        vec(lambda e, ei=ei: e.reduce_sum(grk[:, ei:ei+1],
                                                          gtmp2[:, :NE], axis=AXX))
                    vec(lambda e: e.tensor_scalar(out=grk[:, :NE], in0=grk[:, :NE],
                                                  scalar1=2.5, scalar2=None, op0=ALU.is_lt))
                    vec(lambda e, gg=gg: e.tensor_mul(gg, gg, grk[:, :NE]))
                    vec(lambda e, gg=gg: e.reduce_sum(st128[:, 5:6], gg, axis=AXX))
                    vec(lambda e: e.reciprocal(st128[:, 6:7], st128[:, 5:6]))
                    vec(lambda e, gg=gg: e.tensor_scalar_mul(gg, gg, st128[:, 6:7]), inc=sDV)
                    tp(PT_F[:NE, :128], gg, identf[:, :],
                       waits=[(sDV, sDV.n)] + use("PT_F"), inc=sPE)
                    vec(lambda e, qs=qs: e.tensor_copy(out=gw_fm[:NE, qs*128:(qs+1)*128],
                                                       in_=PT_F[:NE, :128]),
                        waits=[(sPE, sPE.n)], inc=sDV)
                    setok("PT_F", sDV)
                gw_tok = (sDV, sDV.n)
                for ex in range(NES):
                    routed = ex > 0
                    if routed:
                        mm(PB[4][:, :Tt], oneh_sb[:NE, (ex-1)*128:ex*128], gw_fm[:NE, :Tt],
                           start=True, stop=True,
                           waits=[gw_tok] + use("PB4"), inc=sPE)
                        wb_tok = (sPE, sPE.n)
                    t1s, t2s, t3s = {}, {}, {}
                    t1s[0] = wload(wfh_s.ap()[ex*3 + 0][0][:, :],
                                   wfl_s.ap()[ex*3 + 0][0][:, :], 0, queue="sync")
                    t2s[0] = wload(wfh_s.ap()[ex*3 + 1][0][:, :],
                                   wfl_s.ap()[ex*3 + 1][0][:, :], 2, queue="scalar")
                    for pf in range(min(2, NFC)):
                        t3s[pf] = wload(wfh_s.ap()[ex*3 + 2][pf][:, :],
                                        wfl_s.ap()[ex*3 + 2][pf][:, :], 4 + pf,
                                        queue="sync")
                    for fc in range(NFC):
                        s1 = fc % 2
                        s2 = 2 + fc % 2
                        if fc + 1 < NFC:
                            t1s[fc + 1] = wload(wfh_s.ap()[ex*3 + 0][fc+1][:, :],
                                                wfl_s.ap()[ex*3 + 0][fc+1][:, :],
                                                (fc + 1) % 2, queue="sync")
                            t2s[fc + 1] = wload(wfh_s.ap()[ex*3 + 1][fc+1][:, :],
                                                wfl_s.ap()[ex*3 + 1][fc+1][:, :],
                                                2 + (fc + 1) % 2, queue="scalar")
                        t1 = t1s[fc]
                        t2 = t2s[fc]
                        pb1 = PB[fc % 2]
                        pb2 = PB[2 + fc % 2]
                        pairs1 = [(fs_hi[:, s1*D + c*128: s1*D + (c+1)*128],
                                   fs_lo[:, s1*D + c*128: s1*D + (c+1)*128],
                                   zh[:, c*Tt:(c+1)*Tt], zl[:, c*Tt:(c+1)*Tt])
                                  for c in range(NFC)]
                        pairs2 = [(fs_hi[:, s2*D + c*128: s2*D + (c+1)*128],
                                   fs_lo[:, s2*D + c*128: s2*D + (c+1)*128],
                                   zh[:, c*Tt:(c+1)*Tt], zl[:, c*Tt:(c+1)*Tt])
                                  for c in range(NFC)]
                        mm3(pb1[:, :Tt], pairs1, waits_first=[t1] + use(f"PB{fc % 2}"), inc=sPE)
                        setok(f"fsb{s1}", sPE)
                        mm3(pb2[:, :Tt], pairs2,
                            waits_first=([t2] + use(f"PB{2 + fc % 2}") +
                                         ([wb_tok] if routed and fc == 0 else [])), inc=sPE)
                        setok(f"fsb{s2}", sPE)
                        pe_g = (sPE, sPE.n)
                        act(lambda e, pb1=pb1: e.activation(out=t32a[:, :Tt], in_=pb1[:, :Tt],
                                                            func=AF.Silu),
                            waits=[pe_g] + use("t32a"), inc=sAC)
                        setok(f"PB{fc % 2}", sAC)
                        vec(lambda e, pb2=pb2: e.tensor_mul(t32b[:, :Tt], t32a[:, :Tt],
                                                            pb2[:, :Tt]),
                            waits=[(sAC, sAC.n)] + use("t32b"), inc=sDV)
                        setok(f"PB{2 + fc % 2}", sDV)
                        setok("t32a", sDV)
                        if routed:
                            vec(lambda e: e.tensor_mul(t32b[:, :Tt], t32b[:, :Tt], PB[4][:, :Tt]))
                        split32(hh[:, fc*Tt:(fc+1)*Tt], hl[:, fc*Tt:(fc+1)*Tt],
                                t32b[:, :Tt], waits=use("hhl"), inc=sDV)
                        setok("t32b", sDV)
                    h_tok = (sDV, sDV.n)
                    if routed:
                        setok("PB4", sDV)
                    for fc in range(NFC):
                        s3 = 4 + fc % 2
                        t3 = t3s[fc]
                        pb3 = PB[fc % 2]
                        pairs3 = [(fs_hi[:, s3*D + c*128: s3*D + (c+1)*128],
                                   fs_lo[:, s3*D + c*128: s3*D + (c+1)*128],
                                   hh[:, c*Tt:(c+1)*Tt], hl[:, c*Tt:(c+1)*Tt])
                                  for c in range(NFC)]
                        mm3(pb3[:, :Tt], pairs3,
                            waits_first=[t3, h_tok] + use(f"PB{fc % 2}"), inc=sPE)
                        setok(f"fsb{s3}", sPE)
                        if fc + 2 < NFC:
                            t3s[fc + 2] = wload(wfh_s.ap()[ex*3 + 2][fc+2][:, :],
                                                wfl_s.ap()[ex*3 + 2][fc+2][:, :],
                                                s3, queue="sync")
                        if fc == NFC - 1:
                            setok("hhl", sPE)
                        if ex == 0:
                            vec(lambda e, pb3=pb3, fc=fc: e.tensor_copy(
                                    out=acc[:, fc*Tt:(fc+1)*Tt], in_=pb3[:, :Tt]),
                                waits=[(sPE, sPE.n)] + (use("acc") if fc == 0 else []),
                                inc=sDV)
                        else:
                            vec(lambda e, pb3=pb3, fc=fc: e.tensor_add(
                                    acc[:, fc*Tt:(fc+1)*Tt], acc[:, fc*Tt:(fc+1)*Tt],
                                    pb3[:, :Tt]),
                                waits=[(sPE, sPE.n)], inc=sDV)
                        setok(f"PB{fc % 2}", sDV)
                acc_tok = (sDV, sDV.n)
                for fc in range(NFC):
                    vec(lambda e, fc=fc: e.tensor_add(t32b[:, :Tt], acc[:, fc*Tt:(fc+1)*Tt],
                                                      xa[:, fc*Tt:(fc+1)*Tt]),
                        waits=([acc_tok] if fc == 0 else []) + use("t32b"), inc=sDV)
                    store_out(tt, fc)
                    tok["t32b"] = [(sSt, sSt.n)]
                setok("acc", sDV)
                setok("xa", sDV)

        def store_x2(tt, fc):
            dma("gpsimd", x2_s.ap()[tt][:, fc, :], t32b[:, :Tt],
                waits=[(sDV, sDV.n)], inc=sSt)
        ffn(x1_s, p2_store_n, store_x2)
        p3_store_n = sSt.n

        for tt in range(NTT):
            par = tt % 2
            xsem = sXa if par == 0 else sXb
            dma("sync", xa[:].rearrange("p (c t) -> p c t", c=NFC),
                x2_s.ap()[tt][:, :, :], waits=use("xa") + [(sSt, p3_store_n)], inc=xsem)
            xtok = (xsem, xsem.n)
            ztok = layernorm(xa, Tt, xtok)
            for ts in range(Tt // 128):
                tc = tt * (Tt // 128) + ts
                for fc in range(NFC):
                    tp(PT_B[:, :128], zh[:, fc*Tt + ts*128: fc*Tt + ts*128 + 128], identb[:],
                       waits=(([ztok] if fc == 0 else []) + use("PT_B")))
                    tp(PT_B[:, 128:256], zl[:, fc*Tt + ts*128: fc*Tt + ts*128 + 128], identb[:],
                       inc=sPE)
                    vec(lambda e, fc=fc: e.tensor_copy(out=cth[:, fc*128:(fc+1)*128],
                                                       in_=PT_B[:, :128]),
                        waits=[(sPE, sPE.n)] + (use("cthl") if fc == 0 else []))
                    vec(lambda e, fc=fc: e.tensor_copy(out=ctl[:, fc*128:(fc+1)*128],
                                                       in_=PT_B[:, 128:256]), inc=sDV)
                    setok("PT_B", sDV)
                    tp(PT_F[:, :128], xa[:, fc*Tt + ts*128: fc*Tt + ts*128 + 128], identf[:],
                       waits=use("PT_F"), inc=sPE)
                    vec(lambda e, fc=fc: e.tensor_copy(out=ctx32[:, fc*128:(fc+1)*128],
                                                       in_=PT_F[:, :128]),
                        waits=[(sPE, sPE.n)] + (use("ctx32") if fc == 0 else []), inc=sDV)
                    setok("PT_F", sDV)
                zt = (sDV, sDV.n)
                dma("gpsimd", z3h_loc.ap()[tc][:, :], cth[:, :], waits=[zt], inc=sSt)
                dma("gpsimd", z3l_loc.ap()[tc][:, :], ctl[:, :], inc=sSt)
                dma("gpsimd", s_acc.ap()[tc][:, :], ctx32[:, :], inc=sSt)
                tok["cthl"] = [(sSt, sSt.n)]
                tok["ctx32"] = [(sSt, sSt.n)]
            setok("xa", sPE)
        p3b_store_n = sSt.n

        gp(lambda e: e.memset(oblk[0][:, :], 0.0), waits=use("oblk0"), inc=sGP)
        dma("gpsimd", s_acc.ap()[T // 128][0:BS, :], oblk[0][:, :],
            waits=[(sGP, sGP.n)], inc=sSt)
        if 2 * BS <= 128:
            dma("gpsimd", s_acc.ap()[T // 128][BS:2*BS, :], oblk[0][:, :], inc=sSt)
        tok["oblk0"] = [(sSt, sSt.n)]
        dump_n = sSt.n

        p.wait("gpsimd", sSt, dump_n)
        for (src, dst) in ((z3h_loc, z3h_f), (z3l_loc, z3l_f)):
            p.op("gpsimd",
                 lambda e, src=src, dst=dst: e.collective_compute(
                     "AllGather", ALU.bypass,
                     replica_groups=[[0, 1, 2, 3], [4, 5, 6, 7]],
                     ins=[src.ap().opt()], outs=[dst.ap().opt()]),
                 inc=sCC)
        z3_cc = (sCC, sCC.n)
        refresh(["zblk0", "zblk1", "zblk2", "wblk0", "wblk1", "oblk0", "oblk1", "scat",
                 "wblk320", "wblk321", "PB0", "PB1"])

        z3h_flat = z3h_f.ap().rearrange("a p d -> (a p) d")
        z3l_flat = z3l_f.ap().rearrange("a p d -> (a p) d")
        sacc_flat = s_acc.ap().rearrange("a p d -> (a p) d")
        for i in range(3):
            gp(lambda e, i=i: e.memset(zblk_h[i][:, :16], 0.0),
               waits=use(f"zblk{i}") if i == 0 else ())
            gp(lambda e, i=i: e.memset(zblk_l[i][:, :16], 0.0),
               inc=(sGP if i == 2 else None))
        pad_tok = (sGP, sGP.n)

        def conv_gather(j, par, first):
            p.op("gpsimd",
                 lambda e, j=j, par=par: e.indirect_dma_start(
                     out=zblk_h[par][0:BS, 16:16+D], out_offset=None, in_=z3h_flat,
                     in_offset=bass.IndirectOffsetOnAxis(idx_sb[:, 2*j:2*j+1], 0)),
                 waits=(([z3_cc, pad_tok, (sSt, dump_n), setup_ld] if first else []) +
                        use(f"zblk{par}")),
                 inc=sHg[par], n=16)
            p.op("gpsimd",
                 lambda e, j=j, par=par: e.indirect_dma_start(
                     out=zblk_l[par][0:BS, 16:16+D], out_offset=None, in_=z3l_flat,
                     in_offset=bass.IndirectOffsetOnAxis(idx_sb[:, 2*j:2*j+1], 0)),
                 waits=use(f"zblk{par}") if not first else [],
                 inc=sLg[par], n=16)
            g = [(sHg[par], sHg[par].n), (sLg[par], sLg[par].n)]
            dma("sync", zblk_h[par][BS:2*BS, 15:15+D], zblk_h[par][0:BS, 16:16+D],
                waits=g, inc=sG[par])
            dma("sync", zblk_l[par][BS:2*BS, 15:15+D], zblk_l[par][0:BS, 16:16+D],
                inc=sG[par])
            return (sG[par], sG[par].n)

        g_toks = {}
        for pf in range(min(2, M)):
            g_toks[pf] = conv_gather(pf, pf % 3, pf == 0)
        KP = KS // 2
        for j in range(M):
            par = j % 3
            if j + 2 < M:
                g_toks[j + 2] = conv_gather(j + 2, (j + 2) % 3, False)
            g_tok = g_toks[j]
            wpar = j % 2
            semw = sCW[wpar]
            dma("sync", wblk32[wpar][:, :], bvals_h.ap()[j][:, :],
                waits=use(f"wblk32{wpar}"), inc=semw)
            vec(lambda e, wpar=wpar: e.tensor_copy(out=wblk_h[wpar][:, :],
                                                   in_=wblk32[wpar][:, :]),
                waits=[(semw, semw.n)] + use(f"wblk{wpar}"))
            vec(lambda e, wpar=wpar: e.tensor_sub(wblk_l[wpar][:, :], wblk32[wpar][:, :],
                                                  wblk_h[wpar][:, :]), inc=sDV)
            setok(f"wblk32{wpar}", sDV)
            w_tok = (sDV, sDV.n)
            for ts in range(NW5):
                pbk = PB[ts % 2]
                for kp in range(KP):
                    wts = ([g_tok, w_tok] + use(f"PB{ts % 2}")) if kp == 0 else ()
                    mm(pbk[:BS, :W5], wblk_h[wpar][:, kp*BS:(kp+1)*BS],
                       zblk_h[par][:, ts*W5 + 2*kp + 1: ts*W5 + 2*kp + 1 + W5],
                       start=(kp == 0), stop=False, waits=wts)
                    mm(pbk[:BS, :W5], wblk_h[wpar][:, kp*BS:(kp+1)*BS],
                       zblk_l[par][:, ts*W5 + 2*kp + 1: ts*W5 + 2*kp + 1 + W5],
                       start=False, stop=False)
                    mm(pbk[:BS, :W5], wblk_l[wpar][:, kp*BS:(kp+1)*BS],
                       zblk_h[par][:, ts*W5 + 2*kp + 1: ts*W5 + 2*kp + 1 + W5],
                       start=False, stop=(kp == KP - 1), inc=(sPE if kp == KP - 1 else None))
                vec(lambda e, pbk=pbk, ts=ts, j=j: e.tensor_copy(
                        out=oblk[j % 2][:, ts*W5:(ts+1)*W5], in_=pbk[:BS, :W5]),
                    waits=[(sPE, sPE.n)] + (use(f"oblk{j % 2}") if ts == 0 else []),
                    inc=sDV)
                setok(f"PB{ts % 2}", sDV)
            setok(f"zblk{par}", sPE)
            setok(f"wblk{wpar}", sPE)
            o_tok = (sDV, sDV.n)
            p.op("gpsimd",
                 lambda e, j=j: e.indirect_dma_start(
                     out=sacc_flat,
                     out_offset=bass.IndirectOffsetOnAxis(idx_sb[:, 2*j+1:2*j+2], 0),
                     in_=oblk[j % 2][:, :], compute_op=ALU.add, in_offset=None),
                 waits=[o_tok] + use("scat"), inc=sSt, n=16)
            tok["scat"] = [(sSt, sSt.n)]
            tok[f"oblk{j % 2}"] = [(sSt, sSt.n)]
        p4_store_n = sSt.n
        refresh(["xa", "PB0", "PB1", "PT_F", "acc", "z32sq", "t32a", "t32b", "zh",
                 "hhl", "cthl", "ctx32", "PB2", "PB3", "PB4", "PB5"])

        for tt in range(NTT):
            for ts in range(Tt // 128):
                tc = tt * (Tt // 128) + ts
                par = tc % 2
                xsem = sXa if par == 0 else sXb
                dma("sync", xa[:, :D], s_acc.ap()[tc][:, :],
                    waits=use("xa") + ([(sSt, p4_store_n)] if tc == 0 else []), inc=xsem)
                stok = (xsem, xsem.n)
                for fc in range(NFC):
                    tp(PT_F[:, :128], xa[:, fc*128:(fc+1)*128], identf[:],
                       waits=(([stok] if fc == 0 else []) + use("PT_F")), inc=sPE)
                    vec(lambda e, fc=fc, ts=ts: e.tensor_copy(
                            out=acc[:, fc*Tt + ts*128: fc*Tt + ts*128 + 128],
                            in_=PT_F[:, :128]),
                        waits=[(sPE, sPE.n)] + (use("acc") if (ts == 0 and fc == 0) else []),
                        inc=sDV)
                    setok("PT_F", sDV)
                setok("xa", sPE)
            x3tok = (sDV, sDV.n)
            dma("gpsimd", x3_s.ap()[tt][:, :, :],
                acc[:].rearrange("p (c t) -> p c t", c=NFC), waits=[x3tok], inc=sSt)
            tok["acc"] = [(sSt, sSt.n)]
        p4b_store_n = sSt.n

        def store_y(tt, fc):
            dma("gpsimd", y_h.ap()[:, fc, tt*Tt:(tt+1)*Tt], t32b[:, :Tt],
                waits=[(sDV, sDV.n)], inc=sSt)
        ffn(x3_s, p4b_store_n, store_y)

        p.wait("gpsimd", sSt, sSt.n)

    with nc.Block() as block:
        block.sync(lambda e: program(P("sync", e)))
        block.tensor(lambda e: program(P("tensor", e)))
        block.vector(lambda e: program(P("vector", e)))
        block.scalar(lambda e: program(P("scalar", e)))
        block.gpsimd(lambda e: program(P("gpsimd", e)))

    es.close()
    return nc



def _fm(W):
    Din = W.shape[0]
    return np.ascontiguousarray(W.reshape(Din // 128, 128, -1).transpose(1, 0, 2))


def _stripes(W):
    Din, Dout = W.shape
    nfi, nfo = Din // 128, Dout // 128
    return np.ascontiguousarray(
        W.reshape(nfi, 128, nfo, 128).transpose(2, 1, 0, 3).reshape(nfo, 128, Din))


def stage_inputs(cfg, inputs):
    D, S, B, NE, T = cfg.D, cfg.S, cfg.B, cfg.NE, cfg.T
    NFC, BS, KS, M = cfg.NFC, cfg.BS, cfg.KS, cfg.M
    x = np.asarray(inputs["L0_attn_a_hidden_states"], np.float32)
    w_qkv = np.asarray(inputs["L0_attn_a_w_qkv"], np.float32)
    b_qkv = np.asarray(inputs["L0_attn_a_b_qkv"], np.float32)
    w_out = np.asarray(inputs["L0_attn_a_w_out"], np.float32)
    b_out = np.asarray(inputs["L0_attn_a_b_out"], np.float32)
    bvals = np.asarray(inputs["L0_attn_b_block_values"], np.float32)
    cols = np.asarray(inputs["L0_attn_b_cols"], np.int64)
    rows = np.asarray(inputs["L0_attn_b_rows"], np.int64)
    W1 = np.asarray(inputs["L0_ffn_W1"], np.float32)
    Wg = np.asarray(inputs["L0_ffn_Wg"], np.float32)
    W2 = np.asarray(inputs["L0_ffn_W2"], np.float32)
    Ws1 = np.asarray(inputs["L0_ffn_Ws1"], np.float32)
    Wsg = np.asarray(inputs["L0_ffn_Wsg"], np.float32)
    Ws2 = np.asarray(inputs["L0_ffn_Ws2"], np.float32)
    gate_w = np.asarray(inputs["L0_ffn_gate_w"], np.float32)
    gate_b = np.asarray(inputs["L0_ffn_gate_b"], np.float32)
    biases = np.asarray(inputs["L0_ffn_biases"], np.float32)

    wq = _stripes(w_qkv[0:D].T)
    wk = _stripes(w_qkv[D:2*D].T)
    wv = _stripes(w_qkv[2*D:3*D].T)
    wo = _stripes(w_out.T)
    bq = np.ascontiguousarray(b_qkv[0:D].reshape(NFC, 128).T)
    bk = np.ascontiguousarray(b_qkv[D:2*D].reshape(NFC, 128).T)
    bo = np.ascontiguousarray(b_out.reshape(NFC, 128).T)
    bv = np.ascontiguousarray(b_qkv[2*D:3*D].reshape(NFC, 128).T)
    mats = [Ws1, Wsg, Ws2]
    for e in range(NE):
        mats += [W1[e], Wg[e], W2[e]]
    wf = np.ascontiguousarray(np.stack([_stripes(m) for m in mats], axis=0))
    gw = np.ascontiguousarray(gate_w.reshape(NFC, 128, NE).transpose(1, 0, 2))
    gb = np.ascontiguousarray(gate_b.reshape(1, NE))
    gbias = np.ascontiguousarray(biases.reshape(1, NE))
    identb = np.eye(128, dtype=ml_dtypes.bfloat16)
    identf = np.eye(128, dtype=np.float32)
    oneh = np.zeros((NE, NE * 128), np.float32)
    for e in range(NE):
        oneh[e, e*128:(e+1)*128] = 1.0

    nqrt = T // BS
    in_maps = []
    for c in range(8):
        b, qq = divmod(c, 4)
        own = [i for i in range(len(rows)) if qq * nqrt <= rows[i] < (qq + 1) * nqrt]
        assert len(own) <= M
        bv_own = np.zeros((M, 128, (KS // 2) * BS), np.float32)
        colrow = np.zeros((BS, M, 2), np.int32)
        for j in range(M):
            if j < len(own):
                i = own[j]
                arr = bvals[i].transpose(1, 2, 0)
                bv_own[j][0:BS] = arr[:, 0::2, :].reshape(BS, (KS // 2) * BS)
                bv_own[j][BS:2*BS] = arr[:, 1::2, :].reshape(BS, (KS // 2) * BS)
                colrow[:, j, 0] = cols[i] * BS + np.arange(BS)
                colrow[:, j, 1] = (rows[i] - qq * nqrt) * BS + np.arange(BS)
            else:
                colrow[:, j, 0] = np.arange(BS)
                colrow[:, j, 1] = T + np.arange(BS)
        x_own = np.ascontiguousarray(
            x[b].T[:, qq*T:(qq+1)*T].reshape(NFC, 128, T).transpose(1, 0, 2))
        in_maps.append({
            "x_own": x_own, "wq": wq, "wk": wk, "wv": wv, "wo": wo,
            "bq": bq, "bk": bk, "bo": bo, "bv": bv,
            "wf": wf, "gw": gw, "gb": gb, "gbias": gbias,
            "bvals": bv_own, "colrow": colrow,
            "identb": identb, "identf": identf, "oneh": oneh,
        })
    return in_maps


def assemble(cfg, results):
    D, S, B, T, NFC = cfg.D, cfg.S, cfg.B, cfg.T, cfg.NFC
    out = np.empty((B, S, D), np.float32)
    for c in range(8):
        b, qq = divmod(c, 4)
        y = np.asarray(results[c]["y"]).reshape(128, NFC, T)
        out[b, qq*T:(qq+1)*T, :] = y.transpose(1, 0, 2).reshape(D, T).T
    return out


def kernel(**inputs):
    x = np.asarray(inputs["L0_attn_a_hidden_states"])
    B, S, D = x.shape
    NE = int(np.asarray(inputs["L0_ffn_gate_w"]).shape[1])
    NBLK, BS, _, KS = np.asarray(inputs["L0_attn_b_block_values"]).shape
    rows = np.asarray(inputs["L0_attn_b_rows"], np.int64)
    T = B * S // 8
    nqrt = T // BS
    mx = max(int(np.sum((rows >= q * nqrt) & (rows < (q + 1) * nqrt))) for q in range(4))
    M = max(mx, 2)
    cfg = Cfg(D=D, S=S, B=B, NE=NE, NBLK=NBLK, BS=BS, KS=KS, M=M)
    nc = build_nc(cfg)
    in_maps = stage_inputs(cfg, inputs)
    res = run_bass_kernel_spmd(nc, in_maps, core_ids=list(range(8)))
    return assemble(cfg, res.results)
